# version 54
# baseline (speedup 1.0000x reference)
import numpy as np
import ml_dtypes
from contextlib import ExitStack

import concourse.bass as bass
import concourse.mybir as mybir
from concourse.bass_utils import run_bass_kernel_spmd

F32 = mybir.dt.float32
BF16 = mybir.dt.bfloat16
I8 = mybir.dt.int8
AF = mybir.ActivationFunctionType
ALU = mybir.AluOpType

PI = float(np.pi)
MAGIC = 1.5 * 2.0 ** 23
TOK = 1024
KCH = 16
NW = 16
EPS = 1e-5

ENGS = ("sync", "tensor", "vector", "scalar", "gpsimd")
DMA_QUEUES = ("sync", "gpsimd", "scalar")


DMA_RING = 8


class Sched:
    def __init__(self):
        self.ops = {e: [] for e in ENGS}
        self.cnt = {e: 0 for e in ENGS}
        self.dcnt = {q: 0 for q in DMA_QUEUES}
        self.duse = {q: [0] * DMA_RING for q in DMA_QUEUES}
        self.xcnt = {}
        self.clock = {e: {} for e in ENGS}
        self.nops = 0

    def _waits(self, eng, deps):
        best = {}
        for d in deps:
            if d is None:
                continue
            sem_name, val = d
            if val > best.get(sem_name, 0):
                best[sem_name] = val
        w = []
        for sem_name, val in best.items():
            cur = self.clock[eng].get(sem_name, 0)
            if val > cur:
                w.append((sem_name, val))
                self.clock[eng][sem_name] = val
        return w

    def op(self, eng, fn, deps=(), inc=True):
        w = self._waits(eng, deps)
        tok = None
        incs = []
        if inc:
            self.cnt[eng] += 1
            incs = [("e_" + eng, 1)]
            tok = ("e_" + eng, self.cnt[eng])
        self.ops[eng].append((w, fn, incs))
        self.nops += 1
        return tok

    def dma(self, queue, fn, deps=()):
        i = self.dcnt[queue]
        self.dcnt[queue] += 1
        slot = i % DMA_RING
        sem = f"d_{queue}{slot}"
        prev = self.duse[queue][slot]
        deps = list(deps)
        if prev:
            deps.append((sem, 16 * prev))
        self.duse[queue][slot] = prev + 1
        w = self._waits(queue, deps)
        tok = (sem, 16 * (prev + 1))
        self.ops[queue].append((w, fn, [(sem, 16)]))
        self.nops += 1
        return tok

    def op_x(self, eng, fn, sem, amount, deps=()):
        w = self._waits(eng, deps)
        self.xcnt[sem] = self.xcnt.get(sem, 0) + amount
        tok = (sem, self.xcnt[sem])
        self.ops[eng].append((w, fn, [(sem, amount)]))
        self.nops += 1
        return tok

    def wait_only(self, eng, deps):
        w = self._waits(eng, deps)
        if w:
            self.ops[eng].append((w, None, []))

    def barrier_tok(self, eng):
        return ("e_" + eng, self.cnt[eng])

    def emit(self, nc):
        ctx = ExitStack()
        sems = {}
        for e in ENGS:
            sems["e_" + e] = ctx.enter_context(nc.semaphore("e_" + e))
        for q in DMA_QUEUES:
            for slot in range(DMA_RING):
                sems[f"d_{q}{slot}"] = ctx.enter_context(nc.semaphore(f"d_{q}{slot}"))
        for s in self.xcnt:
            sems[s] = ctx.enter_context(nc.semaphore(s))
        with ctx, nc.Block() as block:
            def mk_body(ename):
                def body(eng):
                    for waits, fn, incs in self.ops[ename]:
                        for sem_name, val in waits:
                            eng.wait_ge(sems[sem_name], val)
                        if fn is None:
                            continue
                        inst = fn(eng)
                        for sem_name, amt in incs:
                            inst.then_inc(sems[sem_name], amt)
                return body
            block.sync(mk_body("sync"))
            block.tensor(mk_body("tensor"))
            block.vector(mk_body("vector"))
            block.scalar(mk_body("scalar"))
            block.gpsimd(mk_body("gpsimd"))
        return nc


def _build_program():
    nc = bass.Bass("TRN2", target_bir_lowering=False)

    x_h = nc.dram_tensor("x", [TOK, 2048], F32, kind="ExternalInput")
    w_h = nc.dram_tensor("w", [2048, 4096], F32, kind="ExternalInput")
    wg_h = nc.dram_tensor("wg", [2048, 8192], I8, kind="ExternalInput")
    wu_h = nc.dram_tensor("wu", [2048, 8192], I8, kind="ExternalInput")
    wd_h = nc.dram_tensor("wd", [8192, 2048], I8, kind="ExternalInput")
    bpp_h = nc.dram_tensor("bpp", [128, 32], F32, kind="ExternalInput")
    sg_h = nc.dram_tensor("sgpp", [128, 64], F32, kind="ExternalInput")
    su_h = nc.dram_tensor("supp", [128, 64], F32, kind="ExternalInput")
    sd_h = nc.dram_tensor("sdpp", [128, 16], F32, kind="ExternalInput")
    cost_h = nc.dram_tensor("cost", [1024, TOK], F32, kind="ExternalInput")
    sint_h = nc.dram_tensor("sint", [1024, TOK], F32, kind="ExternalInput")
    idb_h = nc.dram_tensor("idb", [128, 128], BF16, kind="ExternalInput")
    idf_h = nc.dram_tensor("idf", [128, 128], F32, kind="ExternalInput")
    jt_h = nc.dram_tensor("jt", [128, 128], BF16, kind="ExternalInput")
    y_h = nc.dram_tensor("y", [TOK, 2048], F32, kind="ExternalOutput")
    cin_h = nc.dram_tensor("cc_in", [1, 2048], F32, kind="Internal")
    cout_h = nc.dram_tensor("cc_out", [1, 2048], F32, kind="Internal")

    ctx = ExitStack()
    sb = lambda name, shape, dt: ctx.enter_context(nc.sbuf_tensor(name, shape, dt))
    ps = lambda name, shape, dt: ctx.enter_context(nc.psum_tensor(name, shape, dt))

    XN = sb("XN", [128, KCH * TOK], BF16)
    ACC = sb("ACC", [128, KCH * TOK], F32)
    ARN = sb("ARN", [128, 108 * 1024], mybir.dt.uint8)
    BT = sb("BT", [128, 32], F32)
    B2T = sb("B2T", [128, 32], F32)
    SGT = sb("SGT", [128, 64], F32)
    SUT = sb("SUT", [128, 64], F32)
    SDT = sb("SDT", [128, 16], F32)
    IDB = sb("IDB", [128, 128], BF16)
    IDF = sb("IDF", [128, 128], F32)
    JT = sb("JT", [128, 128], BF16)
    STS = sb("STS", [128, 8], F32)
    STQ = sb("STQ", [128, 8], F32)
    MEAN = sb("MEAN", [128, 8], F32)
    RSTD = sb("RSTD", [128, 8], F32)
    TMP8 = sb("TMP8", [128, 8], F32)
    APART = sb("APART", [128, 16], F32)
    EPST = sb("EPST", [128, 1], F32)
    APP = sb("APP", [128, 16], F32)
    JNK = sb("JNKT", [128, 128], BF16)[:, :]
    ATMP = sb("ATMP", [16, 128], F32)

    def carve(off_kib, n_elems, dt):
        nbytes = n_elems * mybir.dt.size(dt)
        return ARN[:, off_kib * 1024: off_kib * 1024 + nbytes].bitcast(dt)

    G = carve(0, 8192, BF16)
    U = carve(16, 8192, BF16)
    D = carve(32, 8192, BF16)
    GS = carve(48, 2048, BF16)
    FS = carve(52, 4096, BF16)
    WK = carve(60, 4096, BF16)
    WV = carve(68, 4096, BF16)
    KSB = carve(76, 2048, F32)
    KB16 = carve(84, 2048, BF16)
    VB16 = carve(88, 2048, BF16)
    AUX0 = carve(92, 1024, F32)
    AUX1 = carve(96, 1024, F32)
    TBL = carve(100, 2048, F32)
    OUTB = carve(60, 16384, BF16)
    XNT = carve(76, 2048, BF16)
    XNT2 = carve(80, 2048, BF16)
    AUXW = carve(84, 2048, F32)
    X2 = [carve(0, 2048, F32), carve(8, 2048, F32)]
    OST = [carve(16, 2048, F32), carve(24, 2048, F32),
           carve(32, 2048, F32), carve(40, 2048, F32)]

    PB = [ps(f"PB{j}", [128, 1024], F32) for j in range(3)]
    P = [PB[j][:, hh * 512:(hh + 1) * 512] for j in range(3) for hh in range(2)]
    PT = [ps("PT0", [128, 1024], BF16), ps("PT1", [128, 1024], BF16)]
    JP = [PT[0][:, :].bitcast(F32), PT[1][:, :].bitcast(F32)]

    S = Sched()

    t_idb = S.dma("scalar", lambda e: e.dma_start(IDB[:, :], idb_h[:, :]))
    XQUE = ["sync", "scalar", "gpsimd", "sync", "scalar", "gpsimd", "sync", "scalar"]
    t_x = []
    for tc in range(8):
        src = x_h[tc * 128:(tc + 1) * 128, :]
        dst = ACC[:, tc * 2048:(tc + 1) * 2048]
        t_x.append(S.dma(XQUE[tc], lambda e, src=src, dst=dst: e.dma_start(dst, src)))

    t_bt = S.dma("sync", lambda e: e.dma_start(BT[:, :], bpp_h[:, :]))
    t_sg = S.dma("sync", lambda e: e.dma_start(SGT[:, :], sg_h[:, :]))
    t_su = S.dma("sync", lambda e: e.dma_start(SUT[:, :], su_h[:, :]))
    t_sd = S.dma("sync", lambda e: e.dma_start(SDT[:, :], sd_h[:, :]))
    t_idf = S.dma("sync", lambda e: e.dma_start(IDF[:, :], idf_h[:, :]))
    t_jt = S.dma("sync", lambda e: e.dma_start(JT[:, :], jt_h[:, :]))
    t_eps = S.op("vector", lambda e: e.memset(EPST[:, :], EPS))
    t_jnk = S.op("vector", lambda e: e.memset(JNK, 0.5))

    def kvblock(dst_slot_ap, col0):
        src3 = w_h[:, col0:col0 + 128].rearrange("(c p) f -> p c f", p=128)
        dst3 = dst_slot_ap.rearrange("p (c f) -> p c f", f=128)
        return lambda e: e.dma_start(dst3, src3)

    def wk_slot(ww):
        return WK[:, ww * 2048:(ww + 1) * 2048]

    def wv_slot(ww):
        return WV[:, ww * 2048:(ww + 1) * 2048]

    def iblock(dst, src_hbm, w):
        src3 = src_hbm[:, w * 512:(w + 1) * 512].rearrange("(c p) f -> p c f", p=128)
        dst3 = dst.rearrange("p (c f) -> p c f", f=512)
        return lambda e: e.dma_start(dst3, src3)

    def dblock(dst, src_hbm, w):
        src3 = src_hbm[w * 512:(w + 1) * 512, :].rearrange("(c p) f -> p c f", p=128)
        dst3 = dst.rearrange("p (c f) -> p c f", f=2048)
        return lambda e: e.dma_start(dst3, src3)

    t_wk = [None, None]
    t_wv = [None, None]
    t_wk[0] = S.dma("gpsimd", kvblock(wk_slot(0), 0))
    t_wv[0] = S.dma("gpsimd", kvblock(wv_slot(0), 2048))
    t_g = S.dma("gpsimd", iblock(G, wg_h, 0))
    t_wk[1] = S.dma("gpsimd", kvblock(wk_slot(1), 128))
    t_wv[1] = S.dma("gpsimd", kvblock(wv_slot(1), 2048 + 128))
    t_u = S.dma("gpsimd", iblock(U, wu_h, 0))
    t_d = S.dma("gpsimd", dblock(D, wd_h, 0))

    bank_free = [None] * 6
    bank_rr = [0]

    def take_bank():
        b = bank_rr[0] % 6
        bank_rr[0] += 1
        return b

    pt_free = [None, None]
    xnt_war = [None, None]
    auxw_war = None
    red_toks = []
    t_rs = [None, None]
    t_mrn = [None, None]
    cp_last = [None] * 8
    xn3 = XN[:, :].rearrange("p (k t) -> p k t", t=TOK)

    def ln_reduce(tc):
        xc = ACC[:, tc * 2048:(tc + 1) * 2048]
        t_s = S.op("vector", lambda e, xc=xc, tc=tc: e.reduce_sum(
            STS[:, tc:tc + 1], xc, axis=mybir.AxisListType.X), deps=[t_x[tc]])
        red_toks.append(t_s)
        warm(3, "spread", extra=t_s)

    def ln_square(tc):
        nonlocal auxw_war
        xc = ACC[:, tc * 2048:(tc + 1) * 2048]
        t_q = S.op("scalar", lambda e, xc=xc, tc=tc: e.activation(
            AUXW, xc, AF.Square, accum_out=STQ[:, tc:tc + 1]),
            deps=[t_x[tc], auxw_war])
        auxw_war = t_q
        red_toks.append(t_q)

    def ln_stats(bq):
        s0 = bq * 4
        sl8 = slice(s0, s0 + 4)
        t_m = S.op("vector", lambda e, sl8=sl8: e.tensor_scalar_mul(
            MEAN[:, sl8], STS[:, sl8], 1.0 / 2048.0),
            deps=list(red_toks))
        t_v2 = S.op("vector", lambda e, sl8=sl8: e.tensor_mul(
            TMP8[:, sl8], MEAN[:, sl8], MEAN[:, sl8]), deps=[t_m])
        t_v3 = S.op("vector", lambda e, sl8=sl8: e.scalar_tensor_tensor(
            TMP8[:, sl8], STQ[:, sl8], 1.0 / 2048.0, TMP8[:, sl8],
            ALU.mult, ALU.subtract), deps=[t_v2])
        t_sq = S.op("scalar", lambda e, sl8=sl8: e.activation(
            STQ[:, sl8], TMP8[:, sl8], AF.Sqrt, bias=EPST[:, :]), deps=[t_v3, t_eps])
        t_rs[bq] = S.op("vector", lambda e, sl8=sl8: e.reciprocal(
            RSTD[:, sl8], STQ[:, sl8]), deps=[t_sq])

    def ln_norm(tc):
        buf = XNT if tc % 2 == 0 else XNT2
        xc = ACC[:, tc * 2048:(tc + 1) * 2048]
        nt = S.op("vector", lambda e, xc=xc, tc=tc, buf=buf: e.tensor_scalar(
            buf, xc, MEAN[:, tc:tc + 1], RSTD[:, tc:tc + 1],
            ALU.subtract, ALU.mult),
            deps=[t_rs[tc // 4], xnt_war[tc % 2], t_x[tc]])
        cpv = cps = None
        for grp in range(4):
            pb = grp % 2
            for c in range(4):
                k = grp * 4 + c
                def tpf(e, k=k, c=c, pb=pb, buf=buf):
                    return e.matmul(PT[pb][:, c * 128:(c + 1) * 128],
                                    buf[:, k * 128:(k + 1) * 128], IDB[:, :],
                                    start=(c == 0), stop=(c == 3), is_transpose=True)
                tp = S.op("tensor", tpf, deps=[nt, t_idb, pt_free[pb]] if c == 0 else (),
                          inc=(c == 3))
            dstv = xn3[:, grp * 4:grp * 4 + 4, tc * 128:(tc + 1) * 128]
            srcv = PT[pb][:, 0:512].rearrange("p (c t) -> p c t", t=128)
            def cpf(e, dstv=dstv, srcv=srcv):
                return e.tensor_copy(dstv, srcv)
            def cpa(e, dstv=dstv, srcv=srcv):
                return e.copy(dstv, srcv)
            if grp == 0:
                cpv = S.op("vector", cpf, deps=[tp])
                pt_free[pb] = cpv
            else:
                cps = S.op("scalar", cpa, deps=[tp])
                pt_free[pb] = cps
        xnt_war[tc % 2] = S.barrier_tok("tensor")
        cp_last[tc] = (cpv, cps)

    def warm(n, tag, extra=None):
        tok = None
        for wi in range(n):
            def wmm(e):
                return e.matmul(P[5][:, 0:128], JNK, JNK, start=True, stop=True)
            tok = S.op("tensor", wmm,
                       deps=[t_jnk] + ([extra] if (extra and wi == 0) else []),
                       inc=(wi == n - 1))
        bank_free[5] = tok
    warm(39, "boot")
    for tc in range(4):
        ln_reduce(tc)
        ln_square(tc)
    ln_stats(0)
    for tc in range(4):
        ln_norm(tc)
        ln_square(tc + 4)
        ln_reduce(tc + 4)
    ln_stats(1)
    for tc in range(4, 8):
        ln_norm(tc)
    t_b2 = S.op("vector", lambda e: e.tensor_scalar_add(B2T[:, :], BT[:, :], PI / 2),
                deps=[t_bt])
    xn_h = [list(cp_last[3]), list(cp_last[7])]
    xn_s = S.barrier_tok("scalar")

    kv_war = [None, None]
    tbl_state = dict(free=None)
    chain_prev = dict(rd=None)

    def kv_mms(wf):
        ww = wf % 2
        rot = wf < 8
        info = dict(wf=wf, ww=ww, rot=rot, ev={})
        info["ev"] = {"K": [None, None], "V": [None, None]}
        for h in range(2):
            for side, wtokl, wsb, dstf, colb in (("K", t_wk, wk_slot(ww), KB16, 0),
                                                 ("V", t_wv, wv_slot(ww), VB16, 16)):
                dst = dstf[:, ww * 1024:(ww + 1) * 1024]
                bk = take_bank()
                deps = [wtokl[ww], bank_free[bk]] + xn_h[h]
                for k in range(KCH):
                    def mm(e, k=k, h=h, bk=bk, wsb=wsb):
                        return e.matmul(P[bk][:, :],
                                        wsb[:, k * 128:(k + 1) * 128],
                                        XN[:, k * TOK + h * 512:k * TOK + h * 512 + 512],
                                        start=(k == 0), stop=(k == KCH - 1))
                    tok = S.op("tensor", mm, deps=deps if k == 0 else (), inc=(k == KCH - 1))
                bias_ap = (BT if rot else B2T)[:, colb + wf: colb + wf + 1]
                def ev(e, h=h, bk=bk, dst=dst, bias_ap=bias_ap):
                    return e.activation(dst[:, h * 512: h * 512 + 512],
                                        P[bk][:, :], AF.Identity, bias=bias_ap)
                evt = S.op("scalar", ev, deps=[tok, t_bt, t_b2, kv_war[ww]])
                bank_free[bk] = evt
                info["ev"][side][h] = evt
        return info

    jp_free = [pt_free[0], pt_free[1]]

    def kv_chain(info):
        wf, ww, rot = info["wf"], info["ww"], info["rot"]
        if rot:
            tt1 = S.dma("sync", lambda e, wf=wf: e.dma_start(
                TBL[:, 0:1024], cost_h[wf * 128:(wf + 1) * 128, :]),
                deps=[tbl_state["free"]])
            tt2 = S.dma("sync", lambda e, wf=wf: e.dma_start(
                TBL[:, 1024:2048], sint_h[wf * 128:(wf + 1) * 128, :]),
                deps=[tbl_state["free"]])
        css = []
        aux_war = chain_prev["rd"]
        ch = KSB[:, ww * 1024:(ww + 1) * 1024]
        for si, side in enumerate(("K", "V")):
            chb = (KB16 if si == 0 else VB16)[:, ww * 1024:(ww + 1) * 1024]
            evs = info["ev"][side]
            scr_war = [css[0]] if si == 1 else []
            if rot:
                jtoks, japs = info["jtok"][si]
                jt_ps = []
                for h in range(2):
                    def tsin(e, h=h, src_ap=japs[h]):
                        return e.tensor_tensor(
                            AUX0[:, h * 512:h * 512 + 512], src_ap,
                            TBL[:, 1024 + h * 512:1024 + h * 512 + 512], ALU.mult)
                    stok = S.op("vector", tsin, deps=[jtoks[h], tt2, aux_war])
                    jt_ps.append(stok)
                if side == "K":
                    jp_free[0] = jt_ps[0]
                    jp_free[1] = jt_ps[1]
                else:
                    bank_free[info["vbank"][0]] = jt_ps[0]
                    bank_free[info["vbank"][1]] = jt_ps[1]
                m1 = S.op("vector", lambda e, chb=chb: e.tensor_tensor(
                    AUX1, chb, TBL[:, 0:1024], ALU.mult), deps=evs + [tt1, tt2, aux_war])
                tbl_state["free"] = m1
                base = S.op("vector", lambda e, ch=ch: e.scalar_tensor_tensor(
                    ch, AUX1, PI / 2, AUX0, ALU.add, ALU.add),
                    deps=[m1] + jt_ps + scr_war)
                r = S.op("vector", lambda e, ch=ch: e.tensor_scalar(
                    AUX0, ch, 1.0 / (2.0 * PI), MAGIC, ALU.mult, ALU.add),
                    deps=[base, aux_war])
                ysrc, ydeps = ch, ()
            else:
                r = S.op("vector", lambda e, chb=chb: e.tensor_scalar(
                    AUX0, chb, 1.0 / (2.0 * PI), MAGIC, ALU.mult, ALU.add),
                    deps=list(evs) + [aux_war])
                ysrc, ydeps = chb, scr_war
            q = S.op("vector", lambda e: e.tensor_scalar(
                AUX1, AUX0, MAGIC, 2.0 * PI, ALU.subtract, ALU.mult), deps=[r])
            y = S.op("vector", lambda e, ch=ch, ysrc=ysrc: e.tensor_sub(
                ch, ysrc, AUX1), deps=[q] + list(ydeps))
            cs = S.op("scalar", lambda e, ch=ch, chb=chb: e.activation(chb, ch, AF.Sin),
                      deps=[y])
            css.append(cs)
            aux_war = y
        kc = KB16[:, ww * 1024:(ww + 1) * 1024]
        vc = VB16[:, ww * 1024:(ww + 1) * 1024]
        pr = S.op("vector", lambda e, kc=kc, vc=vc: e.tensor_tensor(
            AUX0, kc, vc, ALU.mult), deps=css + [aux_war])
        rd = S.op("vector", lambda e, wf=wf: e.reduce_sum(
            APART[:, wf:wf + 1], AUX0, axis=mybir.AxisListType.X), deps=[pr])
        kv_war[ww] = rd
        chain_prev["rd"] = rd

    def kv_jmm_phase(info):
        if not info["rot"]:
            info["jtok"] = None
            return
        ww = info["ww"]
        info["jtok"] = []
        info["vbank"] = []
        for si, (side, dstf) in enumerate((("K", KB16), ("V", VB16))):
            ch = dstf[:, ww * 1024:(ww + 1) * 1024]
            evs = info["ev"][side]
            toks = []
            aps = []
            for h in range(2):
                if side == "K":
                    out_ap = JP[h]
                    war = jp_free[h]
                else:
                    bk = take_bank()
                    out_ap = P[bk][:, :]
                    war = bank_free[bk]
                    info["vbank"].append(bk)
                def jmm(e, ch=ch, h=h, out_ap=out_ap):
                    return e.matmul(out_ap, JT[:, :],
                                    ch[:, h * 512:h * 512 + 512],
                                    start=True, stop=True)
                toks.append(S.op("tensor", jmm, deps=[evs[h], t_jt, war], inc=True))
                aps.append(out_ap)
            info["jtok"].append((toks, aps))

    state = dict(
        fuse_tok=[[None] * 4, [None] * 4],
        fs_read=[None, None],
        acc_last=[[None] * 2 for _ in range(16)],
        t_g=t_g, t_u=t_u, t_d=t_d, t_app=None,
        utok=[None] * 16, o2a=[None] * 16,
    )
    e_tok = [None] * 16
    o2_tok = [[None, None] for _ in range(16)]

    def ffn_window(w, tail_epilogue=False):
        fuse_prev = state["fuse_tok"]
        fuse_tok = [[None] * 4, [None] * 4]
        gate_tok = [[None] * 4 for _ in range(2)]
        g_read_f = u_read_f = None
        for h in range(2):
            for i in range(4):
                bk = take_bank()
                deps = [bank_free[bk]]
                if i == 0:
                    deps += xn_h[h]
                    if h == 0:
                        deps += [state["t_g"]]
                for k in range(KCH):
                    def mm(e, k=k, i=i, h=h, bk=bk):
                        return e.matmul(P[bk][:, :],
                                        G[:, k * 512 + i * 128:k * 512 + (i + 1) * 128],
                                        XN[:, k * TOK + h * 512:k * TOK + h * 512 + 512],
                                        start=(k == 0), stop=(k == KCH - 1))
                    tok = S.op("tensor", mm, deps=deps if k == 0 else (), inc=(k == KCH - 1))
                gs_war = fuse_prev[1][i] if h == 0 else fuse_tok[0][i]
                def silu(e, i=i, bk=bk, w=w):
                    return e.activation(GS[:, i * 512:(i + 1) * 512], P[bk][:, :],
                                        AF.Silu, scale=SGT[:, w * 4 + i:w * 4 + i + 1])
                ev = S.op("scalar", silu, deps=[tok, t_sg, gs_war])
                bank_free[bk] = ev
                gate_tok[h][i] = ev
            if h == 1:
                g_read_f = S.barrier_tok("tensor")
            for i in range(4):
                bk = take_bank()
                deps = [bank_free[bk]]
                if h == 0 and i == 0:
                    deps += [state["t_u"]]
                for k in range(KCH):
                    def mm(e, k=k, i=i, h=h, bk=bk):
                        return e.matmul(P[bk][:, :],
                                        U[:, k * 512 + i * 128:k * 512 + (i + 1) * 128],
                                        XN[:, k * TOK + h * 512:k * TOK + h * 512 + 512],
                                        start=(k == 0), stop=(k == KCH - 1))
                    tok = S.op("tensor", mm, deps=deps if k == 0 else (), inc=(k == KCH - 1))
                def fstt(e, i=i, h=h, bk=bk, w=w):
                    return e.scalar_tensor_tensor(
                        FS[:, i * 1024 + h * 512: i * 1024 + h * 512 + 512],
                        P[bk][:, :], SUT[:, w * 4 + i:w * 4 + i + 1],
                        GS[:, i * 512:(i + 1) * 512], ALU.mult, ALU.mult)
                ev = S.op("vector", fstt, deps=[tok, t_su, gate_tok[h][i],
                                                state["fs_read"][h]])
                bank_free[bk] = ev
                fuse_tok[h][i] = ev
            if h == 1:
                u_read_f = S.barrier_tok("tensor")
        if w + 1 < NW:
            state["t_g"] = S.dma("gpsimd", iblock(G, wg_h, w + 1), deps=[g_read_f])
            state["t_u"] = S.dma("gpsimd", iblock(U, wu_h, w + 1), deps=[u_read_f])
        for dc in range(16):
            j2 = (dc % 3) * 2
            mm_toks = [None, None]
            for h in range(2):
                bk = j2 + h
                for k in range(4):
                    deps = [fuse_tok[h][k]]
                    if k == 0:
                        deps += [bank_free[bk]]
                        if dc == 0 and h == 0:
                            deps += [state["t_d"]]
                    def mm(e, k=k, dc=dc, h=h, bk=bk):
                        return e.matmul(P[bk][:, :],
                                        D[:, k * 2048 + dc * 128:k * 2048 + (dc + 1) * 128],
                                        FS[:, k * TOK + h * 512:k * TOK + h * 512 + 512],
                                        start=(k == 0), stop=(k == 3))
                    tok = S.op("tensor", mm, deps=deps, inc=(k == 3))
                mm_toks[h] = tok
            for h in range(2):
                sl = ACC[:, dc * TOK + h * 512:dc * TOK + h * 512 + 512]
                sl2 = slice(dc * TOK + h * 512, dc * TOK + h * 512 + 512)
                bk = j2 + h
                if w == 0:
                    def accop(e, sl=sl, bk=bk):
                        return e.tensor_copy(sl, P[bk][:, :])
                    dep2 = [mm_toks[h], xn_s]
                elif tail_epilogue:
                    def accop(e, sl2=sl2, dc=dc, bk=bk):
                        return e.scalar_tensor_tensor(
                            OUTB[:, sl2], P[bk][:, :], SDT[:, dc:dc + 1],
                            OUTB[:, sl2], ALU.mult, ALU.add)
                    dep2 = [mm_toks[h], t_sd, state["o2a"][dc]]
                else:
                    def accop(e, sl=sl, bk=bk):
                        return e.tensor_tensor(sl, sl, P[bk][:, :], ALU.add)
                    dep2 = [mm_toks[h], state["acc_last"][dc][h]]
                ev = S.op("vector", accop, deps=dep2)
                bank_free[bk] = ev
                state["acc_last"][dc][h] = ev
                if tail_epilogue:
                    o2_tok[dc][h] = ev
        dmm = S.barrier_tok("tensor")
        state["fs_read"] = [dmm, dmm]
        if w + 1 < NW:
            state["t_d"] = S.dma("gpsimd", dblock(D, wd_h, w + 1), deps=[dmm])
        state["fuse_tok"] = fuse_tok
        return g_read_f, u_read_f

    g15 = u15 = None
    cc = None
    for i in range(16):
        infos = []
        if i < 8:
            for j in (0, 1):
                infos.append(kv_mms(2 * i + j))
            kv_bar = S.barrier_tok("tensor")
            if i + 1 < 8:
                for j in (0, 1):
                    wf = 2 * (i + 1) + j
                    t_wk[wf % 2] = S.dma("gpsimd", kvblock(wk_slot(wf % 2), wf * 128),
                                         deps=[kv_bar])
                    t_wv[wf % 2] = S.dma("gpsimd", kvblock(wv_slot(wf % 2), 2048 + wf * 128),
                                         deps=[kv_bar])
        if infos:
            kv_jmm_phase(infos[0])
            kv_chain(infos[0])
        if i == 15:
            g15, u15 = ffn_window(i, tail_epilogue=True)
        else:
            ffn_window(i)
        if infos:
            kv_jmm_phase(infos[1])
            kv_chain(infos[1])
        if i == 8:
            wk_done = S.barrier_tok("tensor")
            bk = take_bank()
            tA = S.op("tensor", lambda e, bk=bk: e.transpose(
                P[bk][0:16, 0:128], APART[:, :], IDF[:, :]),
                deps=[kv_war[0], kv_war[1], t_idf, bank_free[bk]])
            cA = S.op("vector", lambda e, bk=bk: e.tensor_copy(ATMP[:, :], P[bk][0:16, 0:128]),
                      deps=[tA])
            bank_free[bk] = cA
            dA = S.dma("sync", lambda e: e.dma_start(
                cin_h[0, :].rearrange("(c p) -> c p", p=128), ATMP[:, :]), deps=[cA])
            cc = S.op_x("gpsimd", lambda e: e.collective_compute(
                "AllReduce", ALU.add,
                replica_groups=[[0, 1], [2, 3], [4, 5], [6, 7]],
                ins=[cin_h.ap().opt()], outs=[cout_h.ap().opt()]), "x_cc", 1, deps=[dA])
        if i == 11:
            dB = S.dma("sync", lambda e: e.dma_start(
                ATMP[:, :], cout_h[0, :].rearrange("(c p) -> c p", p=128)), deps=[cc])
            scl = S.op("vector", lambda e: e.tensor_scalar(
                ATMP[:, :], ATMP[:, :], 0.5, 0.5, ALU.mult, ALU.add), deps=[dB])
            bk = take_bank()
            tB = S.op("tensor", lambda e, bk=bk: e.transpose(
                P[bk][0:128, 0:16], ATMP[:, :], IDF[0:16, 0:16]), deps=[scl, bank_free[bk]])
            t_app = S.op("vector", lambda e, bk=bk: e.tensor_copy(APP[:, :], P[bk][0:128, 0:16]),
                         deps=[tB])
            bank_free[bk] = t_app
            state["t_app"] = t_app
        if i == 12:
            mmtok = None
            for tc in range(8):
                def bmm(e, tc=tc):
                    return e.matmul(JP[tc // 4][:, (tc % 4) * 128:(tc % 4 + 1) * 128],
                                    STQ[:, tc:tc + 1].broadcast_to((128, 128)),
                                    IDF[:, :], start=True, stop=True)
                mmtok = S.op("tensor", bmm,
                             deps=[t_idf, chain_prev["rd"]] if tc == 0 else ())
            cp_std = [S.op("vector", lambda e, b=b: e.tensor_copy(
                AUX0[:, b * 512:(b + 1) * 512], JP[b][:, :]), deps=[mmtok])
                for b in range(2)]
            for tc in range(8):
                def bmm2(e, tc=tc):
                    return e.matmul(JP[tc // 4][:, (tc % 4) * 128:(tc % 4 + 1) * 128],
                                    MEAN[:, tc:tc + 1].broadcast_to((128, 128)),
                                    IDF[:, :], start=True, stop=True)
                mmtok = S.op("tensor", bmm2, deps=cp_std if tc == 0 else ())
            cp_m = [S.op("vector", lambda e, b=b: e.tensor_copy(
                AUX1[:, b * 512:(b + 1) * 512], JP[b][:, :]), deps=[mmtok])
                for b in range(2)]
            state["ucopies"] = cp_m
        if 12 <= i <= 14:
            lo, hi = {12: (0, 6), 13: (6, 11), 14: (11, 16)}[i]
            for dc in range(lo, hi):
                sl = slice(dc * TOK, (dc + 1) * TOK)
                u1 = S.op("vector", lambda e, sl=sl, dc=dc: e.scalar_tensor_tensor(
                    OUTB[:, sl], XN[:, sl], APP[:, dc:dc + 1], AUX1[:, :],
                    ALU.mult, ALU.add),
                    deps=[state["t_app"]] + state["ucopies"])
                u2 = S.op("vector", lambda e, sl=sl: e.tensor_tensor(
                    TBL[:, 0:1024], XN[:, sl], AUX0[:, :], ALU.mult),
                    deps=[state.get("ulast"), tbl_state["free"]])
                u3 = S.op("vector", lambda e, sl=sl: e.tensor_tensor(
                    OUTB[:, sl], OUTB[:, sl], TBL[:, 0:1024], ALU.add),
                    deps=[u1, u2])
                state["utok"][dc] = u3
                state["ulast"] = u3
        if i == 14:
            for dc in range(16):
                sl = slice(dc * TOK, (dc + 1) * TOK)
                state["o2a"][dc] = S.op(
                    "vector", lambda e, sl=sl, dc=dc: e.scalar_tensor_tensor(
                        OUTB[:, sl], ACC[:, sl], SDT[:, dc:dc + 1],
                        OUTB[:, sl], ALU.mult, ALU.add),
                    deps=[state["acc_last"][dc][0], state["acc_last"][dc][1],
                          t_sd, state["utok"][dc]])

    ost_free = [[], [], [], []]
    out_toks = []
    PBB = PB[0][:, :].bitcast(BF16)
    tb_ap = [PT[0][:, :], PT[1][:, :], PBB[:, 0:1024], PBB[:, 1024:2048]]
    tb_free = [state["ucopies"][0], state["ucopies"][1],
               S.barrier_tok("vector"), S.barrier_tok("vector")]
    OUTQ = ["sync", "gpsimd", "scalar"]
    for tc in range(8):
        ob = tc % 4
        adds = []
        for b in range(2):
            tb = (tc % 2) * 2 + b
            for j in range(8):
                dc = b * 8 + j
                def tpf(e, dc=dc, j=j, tb=tb, tc=tc):
                    return e.matmul(tb_ap[tb][:, j * 128:(j + 1) * 128],
                                    OUTB[:, dc * TOK + tc * 128: dc * TOK + tc * 128 + 128],
                                    IDB[:, :], start=(j == 0), stop=(j == 7),
                                    is_transpose=True)
                deps = ([o2_tok[dc2][tc // 4] for dc2 in range(b * 8, b * 8 + 8)]
                        + [tb_free[tb], t_idb]) if j == 0 else ()
                tp = S.op("tensor", tpf, deps=deps, inc=(j == 7))
            def cpo(e, tb=tb, b=b, ob=ob):
                return e.tensor_copy(OST[ob][:, b * 1024:(b + 1) * 1024], tb_ap[tb])
            def cpo_s(e, tb=tb, b=b, ob=ob):
                return e.copy(OST[ob][:, b * 1024:(b + 1) * 1024], tb_ap[tb])
            atok = S.op("vector" if b == 0 else "scalar",
                        cpo if b == 0 else cpo_s,
                        deps=[tp] + list(ost_free[ob]))
            tb_free[tb] = atok
            adds.append(atok)
        ot = S.dma(OUTQ[tc % 3], lambda e, tc=tc, ob=ob: e.dma_start(
            y_h[tc * 128:(tc + 1) * 128, :], OST[ob][:, :]), deps=adds)
        out_toks.append(ot)
        ost_free[ob] = [ot]

    S.wait_only("sync", out_toks)

    with ctx:
        S.emit(nc)
    return nc


_NC = None


def _rope_tables():
    rot_dim = 1024
    pos = np.arange(2048, dtype=np.float32)
    inv_freq = (1.0 / (10000.0 ** (np.arange(0, rot_dim, 2, dtype=np.float32) / np.float32(rot_dim)))).astype(np.float32)
    freqs = pos[:, None] * inv_freq[None, :]
    freqs = np.repeat(freqs, 2, axis=-1)
    cos = np.cos(freqs.astype(np.float64)).astype(np.float32)
    sin = np.sin(freqs.astype(np.float64)).astype(np.float32)
    sinT = sin.T.copy()
    sinT[0::2, :] *= -1.0
    return cos.T.copy(), sinT


def _jt_matrix():
    m = np.zeros((128, 128), dtype=ml_dtypes.bfloat16)
    for i in range(64):
        m[2 * i, 2 * i + 1] = 1.0
        m[2 * i + 1, 2 * i] = 1.0
    return m.T.copy()


def kernel(L0_attn_X, L0_attn_W, L0_attn_b, L0_ffn_w_gate, L0_ffn_w_up,
           L0_ffn_w_down, L0_ffn_s_gate, L0_ffn_s_up, L0_ffn_s_down):
    global _NC
    if _NC is None:
        _NC = _build_program()

    X = np.ascontiguousarray(np.asarray(L0_attn_X, dtype=np.float32))
    W = np.ascontiguousarray(np.asarray(L0_attn_W, dtype=np.float32))
    bv = np.asarray(L0_attn_b, dtype=np.float32)
    wg = np.ascontiguousarray(np.asarray(L0_ffn_w_gate, dtype=np.int8))
    wu = np.ascontiguousarray(np.asarray(L0_ffn_w_up, dtype=np.int8))
    wd = np.ascontiguousarray(np.asarray(L0_ffn_w_down, dtype=np.int8))
    sg = np.asarray(L0_ffn_s_gate, dtype=np.float32)
    su = np.asarray(L0_ffn_s_up, dtype=np.float32)
    sd = np.asarray(L0_ffn_s_down, dtype=np.float32)

    cosT, sinT = _rope_tables()
    jt = _jt_matrix()
    idb = np.eye(128, dtype=ml_dtypes.bfloat16)
    idf = np.eye(128, dtype=np.float32)
    bpp = bv.reshape(32, 128).T.copy()
    sgpp = sg.reshape(64, 128).T.copy()
    supp = su.reshape(64, 128).T.copy()
    sdpp = sd.reshape(16, 128).T.copy()

    in_maps = []
    for c in range(8):
        b, h = c // 2, c % 2
        in_maps.append(dict(
            x=np.ascontiguousarray(X[b, h * 1024:(h + 1) * 1024, :]),
            w=W, wg=wg, wu=wu, wd=wd,
            bpp=bpp, sgpp=sgpp, supp=supp, sdpp=sdpp,
            cost=np.ascontiguousarray(cosT[:, h * 1024:(h + 1) * 1024]),
            sint=np.ascontiguousarray(sinT[:, h * 1024:(h + 1) * 1024]),
            idb=idb, idf=idf, jt=jt,
        ))

    res = run_bass_kernel_spmd(_NC, in_maps, core_ids=list(range(8)))
    out = np.empty((4, 2048, 2048), dtype=np.float32)
    for c in range(8):
        b, h = c // 2, c % 2
        out[b, h * 1024:(h + 1) * 1024, :] = np.asarray(res.results[c]["y"])
    return out


# version 55
# speedup vs baseline: 1.0891x; 1.0891x over previous
import numpy as np
import ml_dtypes
from contextlib import ExitStack

import concourse.bass as bass
import concourse.mybir as mybir
from concourse.bass_utils import run_bass_kernel_spmd

F32 = mybir.dt.float32
BF16 = mybir.dt.bfloat16
I8 = mybir.dt.int8
AF = mybir.ActivationFunctionType
ALU = mybir.AluOpType

PI = float(np.pi)
MAGIC = 1.5 * 2.0 ** 23
TOK = 1024
KCH = 16
NW = 16
EPS = 1e-5

ENGS = ("sync", "tensor", "vector", "scalar", "gpsimd")
DMA_QUEUES = ("sync", "gpsimd", "scalar")


DMA_RING = 8


class Sched:
    def __init__(self):
        self.ops = {e: [] for e in ENGS}
        self.cnt = {e: 0 for e in ENGS}
        self.dcnt = {q: 0 for q in DMA_QUEUES}
        self.duse = {q: [0] * DMA_RING for q in DMA_QUEUES}
        self.xcnt = {}
        self.clock = {e: {} for e in ENGS}
        self.nops = 0

    def _waits(self, eng, deps):
        best = {}
        for d in deps:
            if d is None:
                continue
            sem_name, val = d
            if val > best.get(sem_name, 0):
                best[sem_name] = val
        w = []
        for sem_name, val in best.items():
            cur = self.clock[eng].get(sem_name, 0)
            if val > cur:
                w.append((sem_name, val))
                self.clock[eng][sem_name] = val
        return w

    def op(self, eng, fn, deps=(), inc=True):
        w = self._waits(eng, deps)
        tok = None
        incs = []
        if inc:
            self.cnt[eng] += 1
            incs = [("e_" + eng, 1)]
            tok = ("e_" + eng, self.cnt[eng])
        self.ops[eng].append((w, fn, incs))
        self.nops += 1
        return tok

    def dma(self, queue, fn, deps=()):
        i = self.dcnt[queue]
        self.dcnt[queue] += 1
        slot = i % DMA_RING
        sem = f"d_{queue}{slot}"
        prev = self.duse[queue][slot]
        deps = list(deps)
        if prev:
            deps.append((sem, 16 * prev))
        self.duse[queue][slot] = prev + 1
        w = self._waits(queue, deps)
        tok = (sem, 16 * (prev + 1))
        self.ops[queue].append((w, fn, [(sem, 16)]))
        self.nops += 1
        return tok

    def op_x(self, eng, fn, sem, amount, deps=()):
        w = self._waits(eng, deps)
        self.xcnt[sem] = self.xcnt.get(sem, 0) + amount
        tok = (sem, self.xcnt[sem])
        self.ops[eng].append((w, fn, [(sem, amount)]))
        self.nops += 1
        return tok

    def wait_only(self, eng, deps):
        w = self._waits(eng, deps)
        if w:
            self.ops[eng].append((w, None, []))

    def barrier_tok(self, eng):
        return ("e_" + eng, self.cnt[eng])

    def emit(self, nc):
        ctx = ExitStack()
        sems = {}
        for e in ENGS:
            sems["e_" + e] = ctx.enter_context(nc.semaphore("e_" + e))
        for q in DMA_QUEUES:
            for slot in range(DMA_RING):
                sems[f"d_{q}{slot}"] = ctx.enter_context(nc.semaphore(f"d_{q}{slot}"))
        for s in self.xcnt:
            sems[s] = ctx.enter_context(nc.semaphore(s))
        with ctx, nc.Block() as block:
            def mk_body(ename):
                def body(eng):
                    for waits, fn, incs in self.ops[ename]:
                        for sem_name, val in waits:
                            eng.wait_ge(sems[sem_name], val)
                        if fn is None:
                            continue
                        inst = fn(eng)
                        for sem_name, amt in incs:
                            inst.then_inc(sems[sem_name], amt)
                return body
            block.sync(mk_body("sync"))
            block.tensor(mk_body("tensor"))
            block.vector(mk_body("vector"))
            block.scalar(mk_body("scalar"))
            block.gpsimd(mk_body("gpsimd"))
        return nc


def _build_program():
    nc = bass.Bass("TRN2", target_bir_lowering=False)

    x_h = nc.dram_tensor("x", [TOK, 2048], F32, kind="ExternalInput")
    w_h = nc.dram_tensor("w", [2048, 4096], F32, kind="ExternalInput")
    wg_h = nc.dram_tensor("wg", [2048, 8192], I8, kind="ExternalInput")
    wu_h = nc.dram_tensor("wu", [2048, 8192], I8, kind="ExternalInput")
    wd_h = nc.dram_tensor("wd", [8192, 2048], I8, kind="ExternalInput")
    bpp_h = nc.dram_tensor("bpp", [128, 32], F32, kind="ExternalInput")
    sg_h = nc.dram_tensor("sgpp", [128, 64], F32, kind="ExternalInput")
    su_h = nc.dram_tensor("supp", [128, 64], F32, kind="ExternalInput")
    sd_h = nc.dram_tensor("sdpp", [128, 16], F32, kind="ExternalInput")
    cost_h = nc.dram_tensor("cost", [1024, TOK], F32, kind="ExternalInput")
    sint_h = nc.dram_tensor("sint", [1024, TOK], F32, kind="ExternalInput")
    idb_h = nc.dram_tensor("idb", [128, 128], BF16, kind="ExternalInput")
    idf_h = nc.dram_tensor("idf", [128, 128], F32, kind="ExternalInput")
    jt_h = nc.dram_tensor("jt", [128, 128], BF16, kind="ExternalInput")
    y_h = nc.dram_tensor("y", [TOK, 2048], F32, kind="ExternalOutput")
    cin_h = nc.dram_tensor("cc_in", [1, 2048], F32, kind="Internal")
    cout_h = nc.dram_tensor("cc_out", [1, 2048], F32, kind="Internal")

    ctx = ExitStack()
    sb = lambda name, shape, dt: ctx.enter_context(nc.sbuf_tensor(name, shape, dt))
    ps = lambda name, shape, dt: ctx.enter_context(nc.psum_tensor(name, shape, dt))

    XN = sb("XN", [128, KCH * TOK], BF16)
    ACC = sb("ACC", [128, KCH * TOK], F32)
    ARN = sb("ARN", [128, 108 * 1024], mybir.dt.uint8)
    BT = sb("BT", [128, 32], F32)
    B2T = sb("B2T", [128, 32], F32)
    SGT = sb("SGT", [128, 64], F32)
    SUT = sb("SUT", [128, 64], F32)
    SDT = sb("SDT", [128, 16], F32)
    IDB = sb("IDB", [128, 128], BF16)
    IDF = sb("IDF", [128, 128], F32)
    JT = sb("JT", [128, 128], BF16)
    STS = sb("STS", [128, 8], F32)
    STQ = sb("STQ", [128, 8], F32)
    MEAN = sb("MEAN", [128, 8], F32)
    RSTD = sb("RSTD", [128, 8], F32)
    TMP8 = sb("TMP8", [128, 8], F32)
    APART = sb("APART", [128, 16], F32)
    EPST = sb("EPST", [128, 1], F32)
    APP = sb("APP", [128, 16], F32)
    JNK = sb("JNKT", [128, 128], BF16)[:, :]
    ATMP = sb("ATMP", [16, 128], F32)

    def carve(off_kib, n_elems, dt):
        nbytes = n_elems * mybir.dt.size(dt)
        return ARN[:, off_kib * 1024: off_kib * 1024 + nbytes].bitcast(dt)

    G = carve(0, 8192, BF16)
    U = carve(16, 8192, BF16)
    D = carve(32, 8192, BF16)
    GS = carve(48, 2048, BF16)
    FS = carve(52, 4096, BF16)
    WK = carve(60, 4096, BF16)
    WV = carve(68, 4096, BF16)
    KSB = carve(76, 2048, F32)
    KB16 = carve(84, 2048, BF16)
    VB16 = carve(88, 2048, BF16)
    AUX0 = carve(92, 1024, F32)
    AUX1 = carve(96, 1024, F32)
    TBL = carve(100, 2048, F32)
    OUTB = carve(60, 16384, BF16)
    XNT = carve(76, 2048, BF16)
    XNT2 = carve(80, 2048, BF16)
    AUXW = carve(84, 2048, F32)
    X2 = [carve(0, 2048, F32), carve(8, 2048, F32)]
    OST = [carve(16, 2048, F32), carve(24, 2048, F32),
           carve(32, 2048, F32), carve(40, 2048, F32)]

    PB = [ps(f"PB{j}", [128, 1024], F32) for j in range(3)]
    P = [PB[j][:, hh * 512:(hh + 1) * 512] for j in range(3) for hh in range(2)]
    PT = [ps("PT0", [128, 1024], BF16), ps("PT1", [128, 1024], BF16)]
    JP = [PT[0][:, :].bitcast(F32), PT[1][:, :].bitcast(F32)]

    S = Sched()

    t_idb = S.dma("scalar", lambda e: e.dma_start(IDB[:, :], idb_h[:, :]))
    XQUE = ["sync", "gpsimd", "sync", "gpsimd", "sync", "gpsimd", "sync", "gpsimd"]
    t_x = []
    for tc in range(8):
        src = x_h[tc * 128:(tc + 1) * 128, :]
        dst = ACC[:, tc * 2048:(tc + 1) * 2048]
        t_x.append(S.dma(XQUE[tc], lambda e, src=src, dst=dst: e.dma_start(dst, src)))

    t_bt = S.dma("sync", lambda e: e.dma_start(BT[:, :], bpp_h[:, :]))
    t_sg = S.dma("sync", lambda e: e.dma_start(SGT[:, :], sg_h[:, :]))
    t_su = S.dma("sync", lambda e: e.dma_start(SUT[:, :], su_h[:, :]))
    t_sd = S.dma("sync", lambda e: e.dma_start(SDT[:, :], sd_h[:, :]))
    t_idf = S.dma("sync", lambda e: e.dma_start(IDF[:, :], idf_h[:, :]))
    t_jt = S.dma("sync", lambda e: e.dma_start(JT[:, :], jt_h[:, :]))
    t_eps = S.op("vector", lambda e: e.memset(EPST[:, :], EPS))
    t_jnk = S.op("vector", lambda e: e.memset(JNK, 0.5))

    def kvblock(dst_slot_ap, col0):
        src3 = w_h[:, col0:col0 + 128].rearrange("(c p) f -> p c f", p=128)
        dst3 = dst_slot_ap.rearrange("p (c f) -> p c f", f=128)
        return lambda e: e.dma_start(dst3, src3)

    def wk_slot(ww):
        return WK[:, ww * 2048:(ww + 1) * 2048]

    def wv_slot(ww):
        return WV[:, ww * 2048:(ww + 1) * 2048]

    def iblock(dst, src_hbm, w):
        src3 = src_hbm[:, w * 512:(w + 1) * 512].rearrange("(c p) f -> p c f", p=128)
        dst3 = dst.rearrange("p (c f) -> p c f", f=512)
        return lambda e: e.dma_start(dst3, src3)

    def dblock(dst, src_hbm, w):
        src3 = src_hbm[w * 512:(w + 1) * 512, :].rearrange("(c p) f -> p c f", p=128)
        dst3 = dst.rearrange("p (c f) -> p c f", f=2048)
        return lambda e: e.dma_start(dst3, src3)

    t_wk = [None, None]
    t_wv = [None, None]
    t_wk[0] = S.dma("gpsimd", kvblock(wk_slot(0), 0))
    t_wv[0] = S.dma("gpsimd", kvblock(wv_slot(0), 2048))
    t_g = S.dma("gpsimd", iblock(G, wg_h, 0))
    t_wk[1] = S.dma("gpsimd", kvblock(wk_slot(1), 128))
    t_wv[1] = S.dma("gpsimd", kvblock(wv_slot(1), 2048 + 128))
    t_u = S.dma("gpsimd", iblock(U, wu_h, 0))
    t_d = S.dma("gpsimd", dblock(D, wd_h, 0))

    bank_free = [None] * 6
    bank_rr = [0]

    def take_bank():
        b = bank_rr[0] % 6
        bank_rr[0] += 1
        return b

    pt_free = [None, None]
    xnt_war = [None, None]
    auxw_war = None
    red_toks = []
    t_rs = [None, None]
    t_mrn = [None, None]
    cp_last = [None] * 8
    xn3 = XN[:, :].rearrange("p (k t) -> p k t", t=TOK)

    def ln_reduce(tc):
        xc = ACC[:, tc * 2048:(tc + 1) * 2048]
        t_s = S.op("vector", lambda e, xc=xc, tc=tc: e.reduce_sum(
            STS[:, tc:tc + 1], xc, axis=mybir.AxisListType.X), deps=[t_x[tc]])
        red_toks.append(t_s)
        warm(3, "spread", extra=t_s)

    def ln_square(tc):
        nonlocal auxw_war
        xc = ACC[:, tc * 2048:(tc + 1) * 2048]
        t_q = S.op("scalar", lambda e, xc=xc, tc=tc: e.activation(
            AUXW, xc, AF.Square, accum_out=STQ[:, tc:tc + 1]),
            deps=[t_x[tc], auxw_war])
        auxw_war = t_q
        red_toks.append(t_q)

    def ln_stats(bq):
        s0 = bq * 4
        sl8 = slice(s0, s0 + 4)
        t_m = S.op("vector", lambda e, sl8=sl8: e.tensor_scalar_mul(
            MEAN[:, sl8], STS[:, sl8], 1.0 / 2048.0),
            deps=list(red_toks))
        t_v2 = S.op("vector", lambda e, sl8=sl8: e.tensor_mul(
            TMP8[:, sl8], MEAN[:, sl8], MEAN[:, sl8]), deps=[t_m])
        t_v3 = S.op("vector", lambda e, sl8=sl8: e.scalar_tensor_tensor(
            TMP8[:, sl8], STQ[:, sl8], 1.0 / 2048.0, TMP8[:, sl8],
            ALU.mult, ALU.subtract), deps=[t_v2])
        t_sq = S.op("scalar", lambda e, sl8=sl8: e.activation(
            STQ[:, sl8], TMP8[:, sl8], AF.Sqrt, bias=EPST[:, :]), deps=[t_v3, t_eps])
        t_rs[bq] = S.op("vector", lambda e, sl8=sl8: e.reciprocal(
            RSTD[:, sl8], STQ[:, sl8]), deps=[t_sq])

    def ln_norm(tc):
        buf = XNT if tc % 2 == 0 else XNT2
        xc = ACC[:, tc * 2048:(tc + 1) * 2048]
        nt = S.op("vector", lambda e, xc=xc, tc=tc, buf=buf: e.tensor_scalar(
            buf, xc, MEAN[:, tc:tc + 1], RSTD[:, tc:tc + 1],
            ALU.subtract, ALU.mult),
            deps=[t_rs[tc // 4], xnt_war[tc % 2], t_x[tc]])
        cpv = cps = None
        for grp in range(4):
            pb = grp % 2
            for c in range(4):
                k = grp * 4 + c
                def tpf(e, k=k, c=c, pb=pb, buf=buf):
                    return e.matmul(PT[pb][:, c * 128:(c + 1) * 128],
                                    buf[:, k * 128:(k + 1) * 128], IDB[:, :],
                                    start=(c == 0), stop=(c == 3), is_transpose=True)
                tp = S.op("tensor", tpf, deps=[nt, t_idb, pt_free[pb]] if c == 0 else (),
                          inc=(c == 3))
            dstv = xn3[:, grp * 4:grp * 4 + 4, tc * 128:(tc + 1) * 128]
            srcv = PT[pb][:, 0:512].rearrange("p (c t) -> p c t", t=128)
            def cpf(e, dstv=dstv, srcv=srcv):
                return e.tensor_copy(dstv, srcv)
            def cpa(e, dstv=dstv, srcv=srcv):
                return e.copy(dstv, srcv)
            if grp == 0:
                cpv = S.op("vector", cpf, deps=[tp])
                pt_free[pb] = cpv
            else:
                cps = S.op("scalar", cpa, deps=[tp])
                pt_free[pb] = cps
        xnt_war[tc % 2] = S.barrier_tok("tensor")
        cp_last[tc] = (cpv, cps)

    def warm(n, tag, extra=None):
        tok = None
        for wi in range(n):
            def wmm(e):
                return e.matmul(P[5][:, 0:128], JNK, JNK, start=True, stop=True)
            tok = S.op("tensor", wmm,
                       deps=[t_jnk] + ([extra] if (extra and wi == 0) else []),
                       inc=(wi == n - 1))
        bank_free[5] = tok
    warm(39, "boot")
    for tc in range(4):
        ln_reduce(tc)
        ln_square(tc)
    ln_stats(0)
    for tc in range(4):
        ln_norm(tc)
        ln_square(tc + 4)
        ln_reduce(tc + 4)
    ln_stats(1)
    for tc in range(4, 8):
        ln_norm(tc)
    t_b2 = S.op("vector", lambda e: e.tensor_scalar_add(B2T[:, :], BT[:, :], PI / 2),
                deps=[t_bt])
    xn_h = [list(cp_last[3]), list(cp_last[7])]
    xn_s = S.barrier_tok("scalar")

    kv_war = [None, None]
    tbl_state = dict(free=None)
    chain_prev = dict(rd=None)

    def kv_mms(wf):
        ww = wf % 2
        rot = wf < 8
        info = dict(wf=wf, ww=ww, rot=rot, ev={})
        info["ev"] = {"K": [None, None], "V": [None, None]}
        for h in range(2):
            for side, wtokl, wsb, dstf, colb in (("K", t_wk, wk_slot(ww), KB16, 0),
                                                 ("V", t_wv, wv_slot(ww), VB16, 16)):
                dst = dstf[:, ww * 1024:(ww + 1) * 1024]
                bk = take_bank()
                deps = [wtokl[ww], bank_free[bk]] + xn_h[h]
                for k in range(KCH):
                    def mm(e, k=k, h=h, bk=bk, wsb=wsb):
                        return e.matmul(P[bk][:, :],
                                        wsb[:, k * 128:(k + 1) * 128],
                                        XN[:, k * TOK + h * 512:k * TOK + h * 512 + 512],
                                        start=(k == 0), stop=(k == KCH - 1))
                    tok = S.op("tensor", mm, deps=deps if k == 0 else (), inc=(k == KCH - 1))
                bias_ap = (BT if rot else B2T)[:, colb + wf: colb + wf + 1]
                def ev(e, h=h, bk=bk, dst=dst, bias_ap=bias_ap):
                    return e.activation(dst[:, h * 512: h * 512 + 512],
                                        P[bk][:, :], AF.Identity, bias=bias_ap)
                evt = S.op("scalar", ev, deps=[tok, t_bt, t_b2, kv_war[ww]])
                bank_free[bk] = evt
                info["ev"][side][h] = evt
        return info

    jp_free = [pt_free[0], pt_free[1]]

    def kv_chain(info):
        wf, ww, rot = info["wf"], info["ww"], info["rot"]
        if rot:
            tt1 = S.dma("sync", lambda e, wf=wf: e.dma_start(
                TBL[:, 0:1024], cost_h[wf * 128:(wf + 1) * 128, :]),
                deps=[tbl_state["free"]])
            tt2 = S.dma("sync", lambda e, wf=wf: e.dma_start(
                TBL[:, 1024:2048], sint_h[wf * 128:(wf + 1) * 128, :]),
                deps=[tbl_state["free"]])
        css = []
        aux_war = chain_prev["rd"]
        ch = KSB[:, ww * 1024:(ww + 1) * 1024]
        for si, side in enumerate(("K", "V")):
            chb = (KB16 if si == 0 else VB16)[:, ww * 1024:(ww + 1) * 1024]
            evs = info["ev"][side]
            scr_war = [css[0]] if si == 1 else []
            if rot:
                jtoks, japs = info["jtok"][si]
                jt_ps = []
                for h in range(2):
                    def tsin(e, h=h, src_ap=japs[h]):
                        return e.tensor_tensor(
                            AUX0[:, h * 512:h * 512 + 512], src_ap,
                            TBL[:, 1024 + h * 512:1024 + h * 512 + 512], ALU.mult)
                    stok = S.op("vector", tsin, deps=[jtoks[h], tt2, aux_war])
                    jt_ps.append(stok)
                if side == "K":
                    jp_free[0] = jt_ps[0]
                    jp_free[1] = jt_ps[1]
                else:
                    bank_free[info["vbank"][0]] = jt_ps[0]
                    bank_free[info["vbank"][1]] = jt_ps[1]
                m1 = S.op("vector", lambda e, chb=chb: e.tensor_tensor(
                    AUX1, chb, TBL[:, 0:1024], ALU.mult), deps=evs + [tt1, tt2, aux_war])
                tbl_state["free"] = m1
                base = S.op("vector", lambda e, ch=ch: e.scalar_tensor_tensor(
                    ch, AUX1, PI / 2, AUX0, ALU.add, ALU.add),
                    deps=[m1] + jt_ps + scr_war)
                r = S.op("vector", lambda e, ch=ch: e.tensor_scalar(
                    AUX0, ch, 1.0 / (2.0 * PI), MAGIC, ALU.mult, ALU.add),
                    deps=[base, aux_war])
                ysrc, ydeps = ch, ()
            else:
                r = S.op("vector", lambda e, chb=chb: e.tensor_scalar(
                    AUX0, chb, 1.0 / (2.0 * PI), MAGIC, ALU.mult, ALU.add),
                    deps=list(evs) + [aux_war])
                ysrc, ydeps = chb, scr_war
            q = S.op("vector", lambda e: e.tensor_scalar(
                AUX1, AUX0, MAGIC, 2.0 * PI, ALU.subtract, ALU.mult), deps=[r])
            y = S.op("vector", lambda e, ch=ch, ysrc=ysrc: e.tensor_sub(
                ch, ysrc, AUX1), deps=[q] + list(ydeps))
            cs = S.op("scalar", lambda e, ch=ch, chb=chb: e.activation(chb, ch, AF.Sin),
                      deps=[y])
            css.append(cs)
            aux_war = y
        kc = KB16[:, ww * 1024:(ww + 1) * 1024]
        vc = VB16[:, ww * 1024:(ww + 1) * 1024]
        pr = S.op("vector", lambda e, kc=kc, vc=vc: e.tensor_tensor(
            AUX0, kc, vc, ALU.mult), deps=css + [aux_war])
        rd = S.op("vector", lambda e, wf=wf: e.reduce_sum(
            APART[:, wf:wf + 1], AUX0, axis=mybir.AxisListType.X), deps=[pr])
        kv_war[ww] = rd
        chain_prev["rd"] = rd

    def kv_jmm_phase(info):
        if not info["rot"]:
            info["jtok"] = None
            return
        ww = info["ww"]
        info["jtok"] = []
        info["vbank"] = []
        for si, (side, dstf) in enumerate((("K", KB16), ("V", VB16))):
            ch = dstf[:, ww * 1024:(ww + 1) * 1024]
            evs = info["ev"][side]
            toks = []
            aps = []
            for h in range(2):
                if side == "K":
                    out_ap = JP[h]
                    war = jp_free[h]
                else:
                    bk = take_bank()
                    out_ap = P[bk][:, :]
                    war = bank_free[bk]
                    info["vbank"].append(bk)
                def jmm(e, ch=ch, h=h, out_ap=out_ap):
                    return e.matmul(out_ap, JT[:, :],
                                    ch[:, h * 512:h * 512 + 512],
                                    start=True, stop=True)
                toks.append(S.op("tensor", jmm, deps=[evs[h], t_jt, war], inc=True))
                aps.append(out_ap)
            info["jtok"].append((toks, aps))

    state = dict(
        fuse_tok=[[None] * 4, [None] * 4],
        fs_read=[None, None],
        acc_last=[[None] * 2 for _ in range(16)],
        t_g=t_g, t_u=t_u, t_d=t_d, t_app=None,
        utok=[None] * 16, o2a=[None] * 16,
    )
    e_tok = [None] * 16
    o2_tok = [[None, None] for _ in range(16)]

    def ffn_window(w, tail_epilogue=False):
        fuse_prev = state["fuse_tok"]
        fuse_tok = [[None] * 4, [None] * 4]
        gate_tok = [[None] * 4 for _ in range(2)]
        g_read_f = u_read_f = None
        for h in range(2):
            for i in range(4):
                bk = take_bank()
                deps = [bank_free[bk]]
                if i == 0:
                    deps += xn_h[h]
                    if h == 0:
                        deps += [state["t_g"]]
                for k in range(KCH):
                    def mm(e, k=k, i=i, h=h, bk=bk):
                        return e.matmul(P[bk][:, :],
                                        G[:, k * 512 + i * 128:k * 512 + (i + 1) * 128],
                                        XN[:, k * TOK + h * 512:k * TOK + h * 512 + 512],
                                        start=(k == 0), stop=(k == KCH - 1))
                    tok = S.op("tensor", mm, deps=deps if k == 0 else (), inc=(k == KCH - 1))
                gs_war = fuse_prev[1][i] if h == 0 else fuse_tok[0][i]
                def silu(e, i=i, bk=bk, w=w):
                    return e.activation(GS[:, i * 512:(i + 1) * 512], P[bk][:, :],
                                        AF.Silu, scale=SGT[:, w * 4 + i:w * 4 + i + 1])
                ev = S.op("scalar", silu, deps=[tok, t_sg, gs_war])
                bank_free[bk] = ev
                gate_tok[h][i] = ev
            if h == 1:
                g_read_f = S.barrier_tok("tensor")
            for i in range(4):
                bk = take_bank()
                deps = [bank_free[bk]]
                if h == 0 and i == 0:
                    deps += [state["t_u"]]
                for k in range(KCH):
                    def mm(e, k=k, i=i, h=h, bk=bk):
                        return e.matmul(P[bk][:, :],
                                        U[:, k * 512 + i * 128:k * 512 + (i + 1) * 128],
                                        XN[:, k * TOK + h * 512:k * TOK + h * 512 + 512],
                                        start=(k == 0), stop=(k == KCH - 1))
                    tok = S.op("tensor", mm, deps=deps if k == 0 else (), inc=(k == KCH - 1))
                def fstt(e, i=i, h=h, bk=bk, w=w):
                    return e.scalar_tensor_tensor(
                        FS[:, i * 1024 + h * 512: i * 1024 + h * 512 + 512],
                        P[bk][:, :], SUT[:, w * 4 + i:w * 4 + i + 1],
                        GS[:, i * 512:(i + 1) * 512], ALU.mult, ALU.mult)
                ev = S.op("vector", fstt, deps=[tok, t_su, gate_tok[h][i],
                                                state["fs_read"][h]])
                bank_free[bk] = ev
                fuse_tok[h][i] = ev
            if h == 1:
                u_read_f = S.barrier_tok("tensor")
        if w + 1 < NW:
            state["t_g"] = S.dma("gpsimd", iblock(G, wg_h, w + 1), deps=[g_read_f])
            state["t_u"] = S.dma("gpsimd", iblock(U, wu_h, w + 1), deps=[u_read_f])
        for dc in range(16):
            j2 = (dc % 3) * 2
            mm_toks = [None, None]
            for h in range(2):
                bk = j2 + h
                for k in range(4):
                    deps = [fuse_tok[h][k]]
                    if k == 0:
                        deps += [bank_free[bk]]
                        if dc == 0 and h == 0:
                            deps += [state["t_d"]]
                    def mm(e, k=k, dc=dc, h=h, bk=bk):
                        return e.matmul(P[bk][:, :],
                                        D[:, k * 2048 + dc * 128:k * 2048 + (dc + 1) * 128],
                                        FS[:, k * TOK + h * 512:k * TOK + h * 512 + 512],
                                        start=(k == 0), stop=(k == 3))
                    tok = S.op("tensor", mm, deps=deps, inc=(k == 3))
                mm_toks[h] = tok
            for h in range(2):
                sl = ACC[:, dc * TOK + h * 512:dc * TOK + h * 512 + 512]
                sl2 = slice(dc * TOK + h * 512, dc * TOK + h * 512 + 512)
                bk = j2 + h
                if w == 0:
                    def accop(e, sl=sl, bk=bk):
                        return e.tensor_copy(sl, P[bk][:, :])
                    dep2 = [mm_toks[h], xn_s]
                elif tail_epilogue:
                    def accop(e, sl2=sl2, dc=dc, bk=bk):
                        return e.scalar_tensor_tensor(
                            OUTB[:, sl2], P[bk][:, :], SDT[:, dc:dc + 1],
                            OUTB[:, sl2], ALU.mult, ALU.add)
                    dep2 = [mm_toks[h], t_sd, state["o2a"][dc]]
                else:
                    def accop(e, sl=sl, bk=bk):
                        return e.tensor_tensor(sl, sl, P[bk][:, :], ALU.add)
                    dep2 = [mm_toks[h], state["acc_last"][dc][h]]
                ev = S.op("vector", accop, deps=dep2)
                bank_free[bk] = ev
                state["acc_last"][dc][h] = ev
                if tail_epilogue:
                    o2_tok[dc][h] = ev
        dmm = S.barrier_tok("tensor")
        state["fs_read"] = [dmm, dmm]
        if w + 1 < NW:
            state["t_d"] = S.dma("gpsimd", dblock(D, wd_h, w + 1), deps=[dmm])
        state["fuse_tok"] = fuse_tok
        return g_read_f, u_read_f

    g15 = u15 = None
    cc = None
    for i in range(16):
        infos = []
        if i < 8:
            for j in (0, 1):
                infos.append(kv_mms(2 * i + j))
            kv_bar = S.barrier_tok("tensor")
            if i + 1 < 8:
                for j in (0, 1):
                    wf = 2 * (i + 1) + j
                    t_wk[wf % 2] = S.dma("gpsimd", kvblock(wk_slot(wf % 2), wf * 128),
                                         deps=[kv_bar])
                    t_wv[wf % 2] = S.dma("gpsimd", kvblock(wv_slot(wf % 2), 2048 + wf * 128),
                                         deps=[kv_bar])
        if infos:
            kv_jmm_phase(infos[0])
            kv_chain(infos[0])
        if i == 15:
            g15, u15 = ffn_window(i, tail_epilogue=True)
        else:
            ffn_window(i)
        if infos:
            kv_jmm_phase(infos[1])
            kv_chain(infos[1])
        if i == 8:
            wk_done = S.barrier_tok("tensor")
            bk = take_bank()
            tA = S.op("tensor", lambda e, bk=bk: e.transpose(
                P[bk][0:16, 0:128], APART[:, :], IDF[:, :]),
                deps=[kv_war[0], kv_war[1], t_idf, bank_free[bk]])
            cA = S.op("vector", lambda e, bk=bk: e.tensor_copy(ATMP[:, :], P[bk][0:16, 0:128]),
                      deps=[tA])
            bank_free[bk] = cA
            dA = S.dma("sync", lambda e: e.dma_start(
                cin_h[0, :].rearrange("(c p) -> c p", p=128), ATMP[:, :]), deps=[cA])
            cc = S.op_x("gpsimd", lambda e: e.collective_compute(
                "AllReduce", ALU.add,
                replica_groups=[[0, 1], [2, 3], [4, 5], [6, 7]],
                ins=[cin_h.ap().opt()], outs=[cout_h.ap().opt()]), "x_cc", 1, deps=[dA])
        if i == 11:
            dB = S.dma("sync", lambda e: e.dma_start(
                ATMP[:, :], cout_h[0, :].rearrange("(c p) -> c p", p=128)), deps=[cc])
            scl = S.op("vector", lambda e: e.tensor_scalar(
                ATMP[:, :], ATMP[:, :], 0.5, 0.5, ALU.mult, ALU.add), deps=[dB])
            bk = take_bank()
            tB = S.op("tensor", lambda e, bk=bk: e.transpose(
                P[bk][0:128, 0:16], ATMP[:, :], IDF[0:16, 0:16]), deps=[scl, bank_free[bk]])
            t_app = S.op("vector", lambda e, bk=bk: e.tensor_copy(APP[:, :], P[bk][0:128, 0:16]),
                         deps=[tB])
            bank_free[bk] = t_app
            state["t_app"] = t_app
        if i == 12:
            mmtok = None
            for tc in range(8):
                def bmm(e, tc=tc):
                    return e.matmul(JP[tc // 4][:, (tc % 4) * 128:(tc % 4 + 1) * 128],
                                    STQ[:, tc:tc + 1].broadcast_to((128, 128)),
                                    IDF[:, :], start=True, stop=True)
                mmtok = S.op("tensor", bmm,
                             deps=[t_idf, chain_prev["rd"]] if tc == 0 else ())
            cp_std = [S.op("vector", lambda e, b=b: e.tensor_copy(
                AUX0[:, b * 512:(b + 1) * 512], JP[b][:, :]), deps=[mmtok])
                for b in range(2)]
            for tc in range(8):
                def bmm2(e, tc=tc):
                    return e.matmul(JP[tc // 4][:, (tc % 4) * 128:(tc % 4 + 1) * 128],
                                    MEAN[:, tc:tc + 1].broadcast_to((128, 128)),
                                    IDF[:, :], start=True, stop=True)
                mmtok = S.op("tensor", bmm2, deps=cp_std if tc == 0 else ())
            cp_m = [S.op("vector", lambda e, b=b: e.tensor_copy(
                AUX1[:, b * 512:(b + 1) * 512], JP[b][:, :]), deps=[mmtok])
                for b in range(2)]
            state["ucopies"] = cp_m
        if 12 <= i <= 14:
            lo, hi = {12: (0, 6), 13: (6, 11), 14: (11, 16)}[i]
            for dc in range(lo, hi):
                sl = slice(dc * TOK, (dc + 1) * TOK)
                u1 = S.op("vector", lambda e, sl=sl, dc=dc: e.scalar_tensor_tensor(
                    OUTB[:, sl], XN[:, sl], APP[:, dc:dc + 1], AUX1[:, :],
                    ALU.mult, ALU.add),
                    deps=[state["t_app"]] + state["ucopies"])
                u2 = S.op("vector", lambda e, sl=sl: e.tensor_tensor(
                    TBL[:, 0:1024], XN[:, sl], AUX0[:, :], ALU.mult),
                    deps=[state.get("ulast"), tbl_state["free"]])
                u3 = S.op("vector", lambda e, sl=sl: e.tensor_tensor(
                    OUTB[:, sl], OUTB[:, sl], TBL[:, 0:1024], ALU.add),
                    deps=[u1, u2])
                state["utok"][dc] = u3
                state["ulast"] = u3
        if i == 14:
            for dc in range(16):
                sl = slice(dc * TOK, (dc + 1) * TOK)
                state["o2a"][dc] = S.op(
                    "vector", lambda e, sl=sl, dc=dc: e.scalar_tensor_tensor(
                        OUTB[:, sl], ACC[:, sl], SDT[:, dc:dc + 1],
                        OUTB[:, sl], ALU.mult, ALU.add),
                    deps=[state["acc_last"][dc][0], state["acc_last"][dc][1],
                          t_sd, state["utok"][dc]])

    ost_free = [[], [], [], []]
    out_toks = []
    PBB = PB[0][:, :].bitcast(BF16)
    tb_ap = [PT[0][:, :], PT[1][:, :], PBB[:, 0:1024], PBB[:, 1024:2048]]
    tb_free = [state["ucopies"][0], state["ucopies"][1],
               S.barrier_tok("vector"), S.barrier_tok("vector")]
    OUTQ = ["sync", "gpsimd", "scalar"]
    for tc in range(8):
        ob = tc % 4
        adds = []
        for b in range(2):
            tb = (tc % 2) * 2 + b
            for j in range(8):
                dc = b * 8 + j
                def tpf(e, dc=dc, j=j, tb=tb, tc=tc):
                    return e.matmul(tb_ap[tb][:, j * 128:(j + 1) * 128],
                                    OUTB[:, dc * TOK + tc * 128: dc * TOK + tc * 128 + 128],
                                    IDB[:, :], start=(j == 0), stop=(j == 7),
                                    is_transpose=True)
                deps = ([o2_tok[dc2][tc // 4] for dc2 in range(b * 8, b * 8 + 8)]
                        + [tb_free[tb], t_idb]) if j == 0 else ()
                tp = S.op("tensor", tpf, deps=deps, inc=(j == 7))
            def cpo(e, tb=tb, b=b, ob=ob):
                return e.tensor_copy(OST[ob][:, b * 1024:(b + 1) * 1024], tb_ap[tb])
            def cpo_s(e, tb=tb, b=b, ob=ob):
                return e.copy(OST[ob][:, b * 1024:(b + 1) * 1024], tb_ap[tb])
            atok = S.op("vector" if b == 0 else "scalar",
                        cpo if b == 0 else cpo_s,
                        deps=[tp] + list(ost_free[ob]))
            tb_free[tb] = atok
            adds.append(atok)
        ot = S.dma(OUTQ[tc % 3], lambda e, tc=tc, ob=ob: e.dma_start(
            y_h[tc * 128:(tc + 1) * 128, :], OST[ob][:, :]), deps=adds)
        out_toks.append(ot)
        ost_free[ob] = [ot]

    S.wait_only("sync", out_toks)

    with ctx:
        S.emit(nc)
    return nc


_NC = None


def _rope_tables():
    rot_dim = 1024
    pos = np.arange(2048, dtype=np.float32)
    inv_freq = (1.0 / (10000.0 ** (np.arange(0, rot_dim, 2, dtype=np.float32) / np.float32(rot_dim)))).astype(np.float32)
    freqs = pos[:, None] * inv_freq[None, :]
    freqs = np.repeat(freqs, 2, axis=-1)
    cos = np.cos(freqs.astype(np.float64)).astype(np.float32)
    sin = np.sin(freqs.astype(np.float64)).astype(np.float32)
    sinT = sin.T.copy()
    sinT[0::2, :] *= -1.0
    return cos.T.copy(), sinT


def _jt_matrix():
    m = np.zeros((128, 128), dtype=ml_dtypes.bfloat16)
    for i in range(64):
        m[2 * i, 2 * i + 1] = 1.0
        m[2 * i + 1, 2 * i] = 1.0
    return m.T.copy()


def kernel(L0_attn_X, L0_attn_W, L0_attn_b, L0_ffn_w_gate, L0_ffn_w_up,
           L0_ffn_w_down, L0_ffn_s_gate, L0_ffn_s_up, L0_ffn_s_down):
    global _NC
    if _NC is None:
        _NC = _build_program()

    X = np.ascontiguousarray(np.asarray(L0_attn_X, dtype=np.float32))
    W = np.ascontiguousarray(np.asarray(L0_attn_W, dtype=np.float32))
    bv = np.asarray(L0_attn_b, dtype=np.float32)
    wg = np.ascontiguousarray(np.asarray(L0_ffn_w_gate, dtype=np.int8))
    wu = np.ascontiguousarray(np.asarray(L0_ffn_w_up, dtype=np.int8))
    wd = np.ascontiguousarray(np.asarray(L0_ffn_w_down, dtype=np.int8))
    sg = np.asarray(L0_ffn_s_gate, dtype=np.float32)
    su = np.asarray(L0_ffn_s_up, dtype=np.float32)
    sd = np.asarray(L0_ffn_s_down, dtype=np.float32)

    cosT, sinT = _rope_tables()
    jt = _jt_matrix()
    idb = np.eye(128, dtype=ml_dtypes.bfloat16)
    idf = np.eye(128, dtype=np.float32)
    bpp = bv.reshape(32, 128).T.copy()
    sgpp = sg.reshape(64, 128).T.copy()
    supp = su.reshape(64, 128).T.copy()
    sdpp = sd.reshape(16, 128).T.copy()

    in_maps = []
    for c in range(8):
        b, h = c // 2, c % 2
        in_maps.append(dict(
            x=np.ascontiguousarray(X[b, h * 1024:(h + 1) * 1024, :]),
            w=W, wg=wg, wu=wu, wd=wd,
            bpp=bpp, sgpp=sgpp, supp=supp, sdpp=sdpp,
            cost=np.ascontiguousarray(cosT[:, h * 1024:(h + 1) * 1024]),
            sint=np.ascontiguousarray(sinT[:, h * 1024:(h + 1) * 1024]),
            idb=idb, idf=idf, jt=jt,
        ))

    res = run_bass_kernel_spmd(_NC, in_maps, core_ids=list(range(8)))
    out = np.empty((4, 2048, 2048), dtype=np.float32)
    for c in range(8):
        b, h = c // 2, c % 2
        out[b, h * 1024:(h + 1) * 1024, :] = np.asarray(res.results[c]["y"])
    return out


# version 56
# speedup vs baseline: 1.0928x; 1.0034x over previous
import numpy as np
import ml_dtypes
from contextlib import ExitStack

import concourse.bass as bass
import concourse.mybir as mybir
from concourse.bass_utils import run_bass_kernel_spmd

F32 = mybir.dt.float32
BF16 = mybir.dt.bfloat16
I8 = mybir.dt.int8
AF = mybir.ActivationFunctionType
ALU = mybir.AluOpType

PI = float(np.pi)
MAGIC = 1.5 * 2.0 ** 23
TOK = 1024
KCH = 16
NW = 16
EPS = 1e-5

ENGS = ("sync", "tensor", "vector", "scalar", "gpsimd")
DMA_QUEUES = ("sync", "gpsimd", "scalar")


DMA_RING = 8


class Sched:
    def __init__(self):
        self.ops = {e: [] for e in ENGS}
        self.cnt = {e: 0 for e in ENGS}
        self.dcnt = {q: 0 for q in DMA_QUEUES}
        self.duse = {q: [0] * DMA_RING for q in DMA_QUEUES}
        self.xcnt = {}
        self.clock = {e: {} for e in ENGS}
        self.nops = 0

    def _waits(self, eng, deps):
        best = {}
        for d in deps:
            if d is None:
                continue
            sem_name, val = d
            if val > best.get(sem_name, 0):
                best[sem_name] = val
        w = []
        for sem_name, val in best.items():
            cur = self.clock[eng].get(sem_name, 0)
            if val > cur:
                w.append((sem_name, val))
                self.clock[eng][sem_name] = val
        return w

    def op(self, eng, fn, deps=(), inc=True):
        w = self._waits(eng, deps)
        tok = None
        incs = []
        if inc:
            self.cnt[eng] += 1
            incs = [("e_" + eng, 1)]
            tok = ("e_" + eng, self.cnt[eng])
        self.ops[eng].append((w, fn, incs))
        self.nops += 1
        return tok

    def dma(self, queue, fn, deps=()):
        i = self.dcnt[queue]
        self.dcnt[queue] += 1
        slot = i % DMA_RING
        sem = f"d_{queue}{slot}"
        prev = self.duse[queue][slot]
        deps = list(deps)
        if prev:
            deps.append((sem, 16 * prev))
        self.duse[queue][slot] = prev + 1
        w = self._waits(queue, deps)
        tok = (sem, 16 * (prev + 1))
        self.ops[queue].append((w, fn, [(sem, 16)]))
        self.nops += 1
        return tok

    def op_x(self, eng, fn, sem, amount, deps=()):
        w = self._waits(eng, deps)
        self.xcnt[sem] = self.xcnt.get(sem, 0) + amount
        tok = (sem, self.xcnt[sem])
        self.ops[eng].append((w, fn, [(sem, amount)]))
        self.nops += 1
        return tok

    def wait_only(self, eng, deps):
        w = self._waits(eng, deps)
        if w:
            self.ops[eng].append((w, None, []))

    def barrier_tok(self, eng):
        return ("e_" + eng, self.cnt[eng])

    def emit(self, nc):
        ctx = ExitStack()
        sems = {}
        for e in ENGS:
            sems["e_" + e] = ctx.enter_context(nc.semaphore("e_" + e))
        for q in DMA_QUEUES:
            for slot in range(DMA_RING):
                sems[f"d_{q}{slot}"] = ctx.enter_context(nc.semaphore(f"d_{q}{slot}"))
        for s in self.xcnt:
            sems[s] = ctx.enter_context(nc.semaphore(s))
        with ctx, nc.Block() as block:
            def mk_body(ename):
                def body(eng):
                    for waits, fn, incs in self.ops[ename]:
                        for sem_name, val in waits:
                            eng.wait_ge(sems[sem_name], val)
                        if fn is None:
                            continue
                        inst = fn(eng)
                        for sem_name, amt in incs:
                            inst.then_inc(sems[sem_name], amt)
                return body
            block.sync(mk_body("sync"))
            block.tensor(mk_body("tensor"))
            block.vector(mk_body("vector"))
            block.scalar(mk_body("scalar"))
            block.gpsimd(mk_body("gpsimd"))
        return nc


def _build_program():
    nc = bass.Bass("TRN2", target_bir_lowering=False)

    x_h = nc.dram_tensor("x", [TOK, 2048], F32, kind="ExternalInput")
    w_h = nc.dram_tensor("w", [2048, 4096], F32, kind="ExternalInput")
    wg_h = nc.dram_tensor("wg", [2048, 8192], I8, kind="ExternalInput")
    wu_h = nc.dram_tensor("wu", [2048, 8192], I8, kind="ExternalInput")
    wd_h = nc.dram_tensor("wd", [8192, 2048], I8, kind="ExternalInput")
    bpp_h = nc.dram_tensor("bpp", [128, 32], F32, kind="ExternalInput")
    sg_h = nc.dram_tensor("sgpp", [128, 64], F32, kind="ExternalInput")
    su_h = nc.dram_tensor("supp", [128, 64], F32, kind="ExternalInput")
    sd_h = nc.dram_tensor("sdpp", [128, 16], F32, kind="ExternalInput")
    cost_h = nc.dram_tensor("cost", [1024, TOK], F32, kind="ExternalInput")
    sint_h = nc.dram_tensor("sint", [1024, TOK], F32, kind="ExternalInput")
    idb_h = nc.dram_tensor("idb", [128, 128], BF16, kind="ExternalInput")
    idf_h = nc.dram_tensor("idf", [128, 128], F32, kind="ExternalInput")
    jt_h = nc.dram_tensor("jt", [128, 128], BF16, kind="ExternalInput")
    y_h = nc.dram_tensor("y", [TOK, 2048], F32, kind="ExternalOutput")
    cin_h = nc.dram_tensor("cc_in", [1, 2048], F32, kind="Internal")
    cout_h = nc.dram_tensor("cc_out", [1, 2048], F32, kind="Internal")

    ctx = ExitStack()
    sb = lambda name, shape, dt: ctx.enter_context(nc.sbuf_tensor(name, shape, dt))
    ps = lambda name, shape, dt: ctx.enter_context(nc.psum_tensor(name, shape, dt))

    XN = sb("XN", [128, KCH * TOK], BF16)
    ACC = sb("ACC", [128, KCH * TOK], F32)
    ARN = sb("ARN", [128, 108 * 1024], mybir.dt.uint8)
    BT = sb("BT", [128, 32], F32)
    B2T = sb("B2T", [128, 32], F32)
    SGT = sb("SGT", [128, 64], F32)
    SUT = sb("SUT", [128, 64], F32)
    SDT = sb("SDT", [128, 16], F32)
    IDB = sb("IDB", [128, 128], BF16)
    IDF = sb("IDF", [128, 128], F32)
    JT = sb("JT", [128, 128], BF16)
    STS = sb("STS", [128, 8], F32)
    STQ = sb("STQ", [128, 8], F32)
    MEAN = sb("MEAN", [128, 8], F32)
    RSTD = sb("RSTD", [128, 8], F32)
    TMP8 = sb("TMP8", [128, 8], F32)
    APART = sb("APART", [128, 16], F32)
    EPST = sb("EPST", [128, 1], F32)
    APP = sb("APP", [128, 16], F32)
    JNK = sb("JNKT", [128, 128], BF16)[:, :]
    ATMP = sb("ATMP", [16, 128], F32)

    def carve(off_kib, n_elems, dt):
        nbytes = n_elems * mybir.dt.size(dt)
        return ARN[:, off_kib * 1024: off_kib * 1024 + nbytes].bitcast(dt)

    G = carve(0, 8192, BF16)
    U = carve(16, 8192, BF16)
    D = carve(32, 8192, BF16)
    GS = carve(48, 2048, BF16)
    FS = carve(52, 4096, BF16)
    WK = carve(60, 4096, BF16)
    WV = carve(68, 4096, BF16)
    KSB = carve(76, 2048, F32)
    KB16 = carve(84, 2048, BF16)
    VB16 = carve(88, 2048, BF16)
    AUX0 = carve(92, 1024, F32)
    AUX1 = carve(96, 1024, F32)
    TBL = carve(100, 2048, F32)
    OUTB = carve(60, 16384, BF16)
    XNT = carve(76, 2048, BF16)
    XNT2 = carve(80, 2048, BF16)
    AUXW = carve(84, 2048, F32)
    X2 = [carve(0, 2048, F32), carve(8, 2048, F32)]
    OST = [carve(16, 2048, F32), carve(24, 2048, F32),
           carve(32, 2048, F32), carve(40, 2048, F32)]

    PB = [ps(f"PB{j}", [128, 1024], F32) for j in range(3)]
    P = [PB[j][:, hh * 512:(hh + 1) * 512] for j in range(3) for hh in range(2)]
    PT = [ps("PT0", [128, 1024], BF16), ps("PT1", [128, 1024], BF16)]
    JP = [PT[0][:, :].bitcast(F32), PT[1][:, :].bitcast(F32)]

    S = Sched()

    t_idb = S.dma("scalar", lambda e: e.dma_start(IDB[:, :], idb_h[:, :]))
    XQUE = ["sync", "gpsimd", "sync", "gpsimd", "sync", "gpsimd", "sync", "gpsimd"]
    t_x = []
    for tc in range(8):
        src = x_h[tc * 128:(tc + 1) * 128, :]
        dst = ACC[:, tc * 2048:(tc + 1) * 2048]
        t_x.append(S.dma(XQUE[tc], lambda e, src=src, dst=dst: e.dma_start(dst, src)))

    t_bt = S.dma("sync", lambda e: e.dma_start(BT[:, :], bpp_h[:, :]))
    t_sg = S.dma("sync", lambda e: e.dma_start(SGT[:, :], sg_h[:, :]))
    t_su = S.dma("sync", lambda e: e.dma_start(SUT[:, :], su_h[:, :]))
    t_sd = S.dma("sync", lambda e: e.dma_start(SDT[:, :], sd_h[:, :]))
    t_idf = S.dma("sync", lambda e: e.dma_start(IDF[:, :], idf_h[:, :]))
    t_jt = S.dma("sync", lambda e: e.dma_start(JT[:, :], jt_h[:, :]))
    t_eps = S.op("vector", lambda e: e.memset(EPST[:, :], EPS))
    t_jnk = S.op("vector", lambda e: e.memset(JNK, 0.5))

    def kvblock(dst_slot_ap, col0):
        src3 = w_h[:, col0:col0 + 128].rearrange("(c p) f -> p c f", p=128)
        dst3 = dst_slot_ap.rearrange("p (c f) -> p c f", f=128)
        return lambda e: e.dma_start(dst3, src3)

    def wk_slot(ww):
        return WK[:, ww * 2048:(ww + 1) * 2048]

    def wv_slot(ww):
        return WV[:, ww * 2048:(ww + 1) * 2048]

    def iblock(dst, src_hbm, w):
        src3 = src_hbm[:, w * 512:(w + 1) * 512].rearrange("(c p) f -> p c f", p=128)
        dst3 = dst.rearrange("p (c f) -> p c f", f=512)
        return lambda e: e.dma_start(dst3, src3)

    def dblock(dst, src_hbm, w):
        src3 = src_hbm[w * 512:(w + 1) * 512, :].rearrange("(c p) f -> p c f", p=128)
        dst3 = dst.rearrange("p (c f) -> p c f", f=2048)
        return lambda e: e.dma_start(dst3, src3)

    t_wk = [None, None]
    t_wv = [None, None]
    t_wk[0] = S.dma("gpsimd", kvblock(wk_slot(0), 0))
    t_wv[0] = S.dma("gpsimd", kvblock(wv_slot(0), 2048))
    t_g = S.dma("gpsimd", iblock(G, wg_h, 0))
    t_wk[1] = S.dma("gpsimd", kvblock(wk_slot(1), 128))
    t_wv[1] = S.dma("gpsimd", kvblock(wv_slot(1), 2048 + 128))
    t_u = S.dma("gpsimd", iblock(U, wu_h, 0))
    t_d = S.dma("gpsimd", dblock(D, wd_h, 0))

    bank_free = [None] * 6
    bank_rr = [0]

    def take_bank():
        b = bank_rr[0] % 6
        bank_rr[0] += 1
        return b

    pt_free = [None, None]
    xnt_war = [None, None]
    auxw_war = None
    red_toks = []
    t_rs = [None, None]
    t_mrn = [None, None]
    cp_last = [None] * 8
    xn3 = XN[:, :].rearrange("p (k t) -> p k t", t=TOK)

    def ln_reduce(tc):
        xc = ACC[:, tc * 2048:(tc + 1) * 2048]
        t_s = S.op("vector", lambda e, xc=xc, tc=tc: e.reduce_sum(
            STS[:, tc:tc + 1], xc, axis=mybir.AxisListType.X), deps=[t_x[tc]])
        red_toks.append(t_s)
        warm(3, "spread", extra=t_s)

    def ln_square(tc):
        nonlocal auxw_war
        xc = ACC[:, tc * 2048:(tc + 1) * 2048]
        t_q = S.op("scalar", lambda e, xc=xc, tc=tc: e.activation(
            AUXW, xc, AF.Square, accum_out=STQ[:, tc:tc + 1]),
            deps=[t_x[tc], auxw_war])
        auxw_war = t_q
        red_toks.append(t_q)

    def ln_stats(bq):
        s0 = bq * 4
        sl8 = slice(s0, s0 + 4)
        t_m = S.op("vector", lambda e, sl8=sl8: e.tensor_scalar_mul(
            MEAN[:, sl8], STS[:, sl8], 1.0 / 2048.0),
            deps=list(red_toks))
        t_v2 = S.op("vector", lambda e, sl8=sl8: e.tensor_mul(
            TMP8[:, sl8], MEAN[:, sl8], MEAN[:, sl8]), deps=[t_m])
        t_v3 = S.op("vector", lambda e, sl8=sl8: e.scalar_tensor_tensor(
            TMP8[:, sl8], STQ[:, sl8], 1.0 / 2048.0, TMP8[:, sl8],
            ALU.mult, ALU.subtract), deps=[t_v2])
        t_sq = S.op("scalar", lambda e, sl8=sl8: e.activation(
            STQ[:, sl8], TMP8[:, sl8], AF.Sqrt, bias=EPST[:, :]), deps=[t_v3, t_eps])
        t_rs[bq] = S.op("vector", lambda e, sl8=sl8: e.reciprocal(
            RSTD[:, sl8], STQ[:, sl8]), deps=[t_sq])

    def ln_norm(tc):
        buf = XNT if tc % 2 == 0 else XNT2
        xc = ACC[:, tc * 2048:(tc + 1) * 2048]
        nt = S.op("vector", lambda e, xc=xc, tc=tc, buf=buf: e.tensor_scalar(
            buf, xc, MEAN[:, tc:tc + 1], RSTD[:, tc:tc + 1],
            ALU.subtract, ALU.mult),
            deps=[t_rs[tc // 4], xnt_war[tc % 2], t_x[tc]])
        cpv = cps = None
        for grp in range(4):
            pb = grp % 2
            for c in range(4):
                k = grp * 4 + c
                def tpf(e, k=k, c=c, pb=pb, buf=buf):
                    return e.matmul(PT[pb][:, c * 128:(c + 1) * 128],
                                    buf[:, k * 128:(k + 1) * 128], IDB[:, :],
                                    start=(c == 0), stop=(c == 3), is_transpose=True)
                tp = S.op("tensor", tpf, deps=[nt, t_idb, pt_free[pb]] if c == 0 else (),
                          inc=(c == 3))
            dstv = xn3[:, grp * 4:grp * 4 + 4, tc * 128:(tc + 1) * 128]
            srcv = PT[pb][:, 0:512].rearrange("p (c t) -> p c t", t=128)
            def cpf(e, dstv=dstv, srcv=srcv):
                return e.tensor_copy(dstv, srcv)
            def cpa(e, dstv=dstv, srcv=srcv):
                return e.copy(dstv, srcv)
            if grp == 0:
                cpv = S.op("vector", cpf, deps=[tp])
                pt_free[pb] = cpv
            else:
                cps = S.op("scalar", cpa, deps=[tp])
                pt_free[pb] = cps
        xnt_war[tc % 2] = S.barrier_tok("tensor")
        cp_last[tc] = (cpv, cps)

    def warm(n, tag, extra=None):
        tok = None
        for wi in range(n):
            def wmm(e):
                return e.matmul(P[5][:, 0:128], JNK, JNK, start=True, stop=True)
            tok = S.op("tensor", wmm,
                       deps=[t_jnk] + ([extra] if (extra and wi == 0) else []),
                       inc=(wi == n - 1))
        bank_free[5] = tok
    warm(39, "boot")
    for tc in range(4):
        ln_reduce(tc)
        ln_square(tc)
    ln_stats(0)
    for tc in range(4):
        ln_norm(tc)
        ln_square(tc + 4)
        ln_reduce(tc + 4)
    ln_stats(1)
    for tc in range(4, 8):
        ln_norm(tc)
    t_b2 = S.op("vector", lambda e: e.tensor_scalar_add(B2T[:, :], BT[:, :], PI / 2),
                deps=[t_bt])
    xn_h = [list(cp_last[3]), list(cp_last[7])]
    xn_s = S.barrier_tok("scalar")

    kv_war = [None, None]
    tbl_state = dict(free=None)
    chain_prev = dict(rd=None)

    def kv_mms(wf):
        ww = wf % 2
        rot = wf < 8
        info = dict(wf=wf, ww=ww, rot=rot, ev={})
        info["ev"] = {"K": [None, None], "V": [None, None]}
        for h in range(2):
            for side, wtokl, wsb, dstf, colb in (("K", t_wk, wk_slot(ww), KB16, 0),
                                                 ("V", t_wv, wv_slot(ww), VB16, 16)):
                dst = dstf[:, ww * 1024:(ww + 1) * 1024]
                bk = take_bank()
                deps = [wtokl[ww], bank_free[bk]] + xn_h[h]
                for k in range(KCH):
                    def mm(e, k=k, h=h, bk=bk, wsb=wsb):
                        return e.matmul(P[bk][:, :],
                                        wsb[:, k * 128:(k + 1) * 128],
                                        XN[:, k * TOK + h * 512:k * TOK + h * 512 + 512],
                                        start=(k == 0), stop=(k == KCH - 1))
                    tok = S.op("tensor", mm, deps=deps if k == 0 else (), inc=(k == KCH - 1))
                bias_ap = (BT if rot else B2T)[:, colb + wf: colb + wf + 1]
                def ev(e, h=h, bk=bk, dst=dst, bias_ap=bias_ap):
                    return e.activation(dst[:, h * 512: h * 512 + 512],
                                        P[bk][:, :], AF.Identity, bias=bias_ap)
                evt = S.op("scalar", ev, deps=[tok, t_bt, t_b2, kv_war[ww]])
                bank_free[bk] = evt
                info["ev"][side][h] = evt
        return info

    jp_free = [pt_free[0], pt_free[1]]

    def kv_chain(info):
        wf, ww, rot = info["wf"], info["ww"], info["rot"]
        if rot:
            tt1 = S.dma("sync", lambda e, wf=wf: e.dma_start(
                TBL[:, 0:1024], cost_h[wf * 128:(wf + 1) * 128, :]),
                deps=[tbl_state["free"]])
            tt2 = S.dma("sync", lambda e, wf=wf: e.dma_start(
                TBL[:, 1024:2048], sint_h[wf * 128:(wf + 1) * 128, :]),
                deps=[tbl_state["free"]])
        css = []
        aux_war = chain_prev["rd"]
        ch = KSB[:, ww * 1024:(ww + 1) * 1024]
        for si, side in enumerate(("K", "V")):
            chb = (KB16 if si == 0 else VB16)[:, ww * 1024:(ww + 1) * 1024]
            evs = info["ev"][side]
            scr_war = [css[0]] if si == 1 else []
            if rot:
                jtoks, japs = info["jtok"][si]
                jt_ps = []
                for h in range(2):
                    def tsin(e, h=h, src_ap=japs[h]):
                        return e.tensor_tensor(
                            AUX0[:, h * 512:h * 512 + 512], src_ap,
                            TBL[:, 1024 + h * 512:1024 + h * 512 + 512], ALU.mult)
                    stok = S.op("vector", tsin, deps=[jtoks[h], tt2, aux_war])
                    jt_ps.append(stok)
                if side == "K":
                    jp_free[0] = jt_ps[0]
                    jp_free[1] = jt_ps[1]
                else:
                    bank_free[info["vbank"][0]] = jt_ps[0]
                    bank_free[info["vbank"][1]] = jt_ps[1]
                m1 = S.op("vector", lambda e, chb=chb: e.tensor_tensor(
                    AUX1, chb, TBL[:, 0:1024], ALU.mult), deps=evs + [tt1, tt2, aux_war])
                tbl_state["free"] = m1
                base = S.op("vector", lambda e, ch=ch: e.scalar_tensor_tensor(
                    ch, AUX1, PI / 2, AUX0, ALU.add, ALU.add),
                    deps=[m1] + jt_ps + scr_war)
                r = S.op("vector", lambda e, ch=ch: e.tensor_scalar(
                    AUX0, ch, 1.0 / (2.0 * PI), MAGIC, ALU.mult, ALU.add),
                    deps=[base, aux_war])
                ysrc, ydeps = ch, ()
            else:
                r = S.op("vector", lambda e, chb=chb: e.tensor_scalar(
                    AUX0, chb, 1.0 / (2.0 * PI), MAGIC, ALU.mult, ALU.add),
                    deps=list(evs) + [aux_war])
                ysrc, ydeps = chb, scr_war
            q = S.op("vector", lambda e: e.tensor_scalar(
                AUX1, AUX0, MAGIC, 2.0 * PI, ALU.subtract, ALU.mult), deps=[r])
            y = S.op("vector", lambda e, ch=ch, ysrc=ysrc: e.tensor_sub(
                ch, ysrc, AUX1), deps=[q] + list(ydeps))
            cs = S.op("scalar", lambda e, ch=ch, chb=chb: e.activation(chb, ch, AF.Sin),
                      deps=[y])
            css.append(cs)
            aux_war = y
        kc = KB16[:, ww * 1024:(ww + 1) * 1024]
        vc = VB16[:, ww * 1024:(ww + 1) * 1024]
        pr = S.op("vector", lambda e, kc=kc, vc=vc: e.tensor_tensor(
            AUX0, kc, vc, ALU.mult), deps=css + [aux_war])
        rd = S.op("vector", lambda e, wf=wf: e.reduce_sum(
            APART[:, wf:wf + 1], AUX0, axis=mybir.AxisListType.X), deps=[pr])
        kv_war[ww] = rd
        chain_prev["rd"] = rd

    def kv_jmm_phase(info):
        if not info["rot"]:
            info["jtok"] = None
            return
        ww = info["ww"]
        info["jtok"] = []
        info["vbank"] = []
        for si, (side, dstf) in enumerate((("K", KB16), ("V", VB16))):
            ch = dstf[:, ww * 1024:(ww + 1) * 1024]
            evs = info["ev"][side]
            toks = []
            aps = []
            for h in range(2):
                if side == "K":
                    out_ap = JP[h]
                    war = jp_free[h]
                else:
                    bk = take_bank()
                    out_ap = P[bk][:, :]
                    war = bank_free[bk]
                    info["vbank"].append(bk)
                def jmm(e, ch=ch, h=h, out_ap=out_ap):
                    return e.matmul(out_ap, JT[:, :],
                                    ch[:, h * 512:h * 512 + 512],
                                    start=True, stop=True)
                toks.append(S.op("tensor", jmm, deps=[evs[h], t_jt, war], inc=True))
                aps.append(out_ap)
            info["jtok"].append((toks, aps))

    state = dict(
        fuse_tok=[[None] * 4, [None] * 4],
        fs_read=[None, None],
        acc_last=[[None] * 2 for _ in range(16)],
        t_g=t_g, t_u=t_u, t_d=t_d, t_app=None,
        utok=[None] * 16, o2a=[None] * 16,
    )
    e_tok = [None] * 16
    o2_tok = [[None, None] for _ in range(16)]

    def ffn_window(w, tail_epilogue=False):
        fuse_prev = state["fuse_tok"]
        fuse_tok = [[None] * 4, [None] * 4]
        gate_tok = [[None] * 4 for _ in range(2)]
        g_read_f = u_read_f = None
        for h in range(2):
            for i in range(4):
                bk = take_bank()
                deps = [bank_free[bk]]
                if i == 0:
                    deps += xn_h[h]
                    if h == 0:
                        deps += [state["t_g"]]
                for k in range(KCH):
                    def mm(e, k=k, i=i, h=h, bk=bk):
                        return e.matmul(P[bk][:, :],
                                        G[:, k * 512 + i * 128:k * 512 + (i + 1) * 128],
                                        XN[:, k * TOK + h * 512:k * TOK + h * 512 + 512],
                                        start=(k == 0), stop=(k == KCH - 1))
                    tok = S.op("tensor", mm, deps=deps if k == 0 else (), inc=(k == KCH - 1))
                gs_war = fuse_prev[1][i] if h == 0 else fuse_tok[0][i]
                def silu(e, i=i, bk=bk, w=w):
                    return e.activation(GS[:, i * 512:(i + 1) * 512], P[bk][:, :],
                                        AF.Silu, scale=SGT[:, w * 4 + i:w * 4 + i + 1])
                ev = S.op("scalar", silu, deps=[tok, t_sg, gs_war])
                bank_free[bk] = ev
                gate_tok[h][i] = ev
            if h == 1:
                g_read_f = S.barrier_tok("tensor")
            for i in range(4):
                bk = take_bank()
                deps = [bank_free[bk]]
                if h == 0 and i == 0:
                    deps += [state["t_u"]]
                for k in range(KCH):
                    def mm(e, k=k, i=i, h=h, bk=bk):
                        return e.matmul(P[bk][:, :],
                                        U[:, k * 512 + i * 128:k * 512 + (i + 1) * 128],
                                        XN[:, k * TOK + h * 512:k * TOK + h * 512 + 512],
                                        start=(k == 0), stop=(k == KCH - 1))
                    tok = S.op("tensor", mm, deps=deps if k == 0 else (), inc=(k == KCH - 1))
                def fstt(e, i=i, h=h, bk=bk, w=w):
                    return e.scalar_tensor_tensor(
                        FS[:, i * 1024 + h * 512: i * 1024 + h * 512 + 512],
                        P[bk][:, :], SUT[:, w * 4 + i:w * 4 + i + 1],
                        GS[:, i * 512:(i + 1) * 512], ALU.mult, ALU.mult)
                ev = S.op("vector", fstt, deps=[tok, t_su, gate_tok[h][i],
                                                state["fs_read"][h]])
                bank_free[bk] = ev
                fuse_tok[h][i] = ev
            if h == 1:
                u_read_f = S.barrier_tok("tensor")
        if w + 1 < NW:
            state["t_g"] = S.dma("gpsimd", iblock(G, wg_h, w + 1), deps=[g_read_f])
            state["t_u"] = S.dma("gpsimd", iblock(U, wu_h, w + 1), deps=[u_read_f])
        for dc in range(16):
            j2 = (dc % 3) * 2
            mm_toks = [None, None]
            for h in range(2):
                bk = j2 + h
                for k in range(4):
                    deps = [fuse_tok[h][k]]
                    if k == 0:
                        deps += [bank_free[bk]]
                        if dc == 0 and h == 0:
                            deps += [state["t_d"]]
                    def mm(e, k=k, dc=dc, h=h, bk=bk):
                        return e.matmul(P[bk][:, :],
                                        D[:, k * 2048 + dc * 128:k * 2048 + (dc + 1) * 128],
                                        FS[:, k * TOK + h * 512:k * TOK + h * 512 + 512],
                                        start=(k == 0), stop=(k == 3))
                    tok = S.op("tensor", mm, deps=deps, inc=(k == 3))
                mm_toks[h] = tok
            for h in range(2):
                sl = ACC[:, dc * TOK + h * 512:dc * TOK + h * 512 + 512]
                sl2 = slice(dc * TOK + h * 512, dc * TOK + h * 512 + 512)
                bk = j2 + h
                if w == 0:
                    def accop(e, sl=sl, bk=bk):
                        return e.tensor_copy(sl, P[bk][:, :])
                    dep2 = [mm_toks[h], xn_s]
                elif tail_epilogue:
                    def accop(e, sl2=sl2, dc=dc, bk=bk):
                        return e.scalar_tensor_tensor(
                            OUTB[:, sl2], P[bk][:, :], SDT[:, dc:dc + 1],
                            OUTB[:, sl2], ALU.mult, ALU.add)
                    dep2 = [mm_toks[h], t_sd, state["o2a"][dc]]
                else:
                    def accop(e, sl=sl, bk=bk):
                        return e.tensor_tensor(sl, sl, P[bk][:, :], ALU.add)
                    dep2 = [mm_toks[h], state["acc_last"][dc][h]]
                ev = S.op("vector", accop, deps=dep2)
                bank_free[bk] = ev
                state["acc_last"][dc][h] = ev
                if tail_epilogue:
                    o2_tok[dc][h] = ev
        dmm = S.barrier_tok("tensor")
        state["fs_read"] = [dmm, dmm]
        if w + 1 < NW:
            state["t_d"] = S.dma("gpsimd", dblock(D, wd_h, w + 1), deps=[dmm])
        state["fuse_tok"] = fuse_tok
        return g_read_f, u_read_f

    g15 = u15 = None
    cc = None
    for i in range(16):
        infos = []
        if i < 8:
            for j in (0, 1):
                infos.append(kv_mms(2 * i + j))
            kv_bar = S.barrier_tok("tensor")
            if i + 1 < 8:
                for j in (0, 1):
                    wf = 2 * (i + 1) + j
                    t_wk[wf % 2] = S.dma("gpsimd", kvblock(wk_slot(wf % 2), wf * 128),
                                         deps=[kv_bar])
                    t_wv[wf % 2] = S.dma("gpsimd", kvblock(wv_slot(wf % 2), 2048 + wf * 128),
                                         deps=[kv_bar])
        if infos:
            kv_jmm_phase(infos[0])
            kv_chain(infos[0])
        if i == 15:
            g15, u15 = ffn_window(i, tail_epilogue=True)
        else:
            ffn_window(i)
        if infos:
            kv_jmm_phase(infos[1])
            kv_chain(infos[1])
        if i == 8:
            wk_done = S.barrier_tok("tensor")
            bk = take_bank()
            tA = S.op("tensor", lambda e, bk=bk: e.transpose(
                P[bk][0:16, 0:128], APART[:, :], IDF[:, :]),
                deps=[kv_war[0], kv_war[1], t_idf, bank_free[bk]])
            cA = S.op("vector", lambda e, bk=bk: e.tensor_copy(ATMP[:, :], P[bk][0:16, 0:128]),
                      deps=[tA])
            bank_free[bk] = cA
            dA = S.dma("sync", lambda e: e.dma_start(
                cin_h[0, :].rearrange("(c p) -> c p", p=128), ATMP[:, :]), deps=[cA])
            cc = S.op_x("gpsimd", lambda e: e.collective_compute(
                "AllReduce", ALU.add,
                replica_groups=[[0, 1], [2, 3], [4, 5], [6, 7]],
                ins=[cin_h.ap().opt()], outs=[cout_h.ap().opt()]), "x_cc", 1, deps=[dA])
        if i == 11:
            dB = S.dma("sync", lambda e: e.dma_start(
                ATMP[:, :], cout_h[0, :].rearrange("(c p) -> c p", p=128)), deps=[cc])
            scl = S.op("vector", lambda e: e.tensor_scalar(
                ATMP[:, :], ATMP[:, :], 0.5, 0.5, ALU.mult, ALU.add), deps=[dB])
            bk = take_bank()
            tB = S.op("tensor", lambda e, bk=bk: e.transpose(
                P[bk][0:128, 0:16], ATMP[:, :], IDF[0:16, 0:16]), deps=[scl, bank_free[bk]])
            t_app = S.op("vector", lambda e, bk=bk: e.tensor_copy(APP[:, :], P[bk][0:128, 0:16]),
                         deps=[tB])
            bank_free[bk] = t_app
            state["t_app"] = t_app
        if i == 12:
            mmtok = None
            for tc in range(8):
                def bmm(e, tc=tc):
                    return e.matmul(JP[tc // 4][:, (tc % 4) * 128:(tc % 4 + 1) * 128],
                                    STQ[:, tc:tc + 1].broadcast_to((128, 128)),
                                    IDF[:, :], start=True, stop=True)
                mmtok = S.op("tensor", bmm,
                             deps=[t_idf, chain_prev["rd"]] if tc == 0 else ())
            cp_std = [S.op("vector", lambda e, b=b: e.tensor_copy(
                AUX0[:, b * 512:(b + 1) * 512], JP[b][:, :]), deps=[mmtok])
                for b in range(2)]
            for tc in range(8):
                def bmm2(e, tc=tc):
                    return e.matmul(JP[tc // 4][:, (tc % 4) * 128:(tc % 4 + 1) * 128],
                                    MEAN[:, tc:tc + 1].broadcast_to((128, 128)),
                                    IDF[:, :], start=True, stop=True)
                mmtok = S.op("tensor", bmm2, deps=cp_std if tc == 0 else ())
            cp_m = [S.op("vector", lambda e, b=b: e.tensor_copy(
                AUX1[:, b * 512:(b + 1) * 512], JP[b][:, :]), deps=[mmtok])
                for b in range(2)]
            state["ucopies"] = cp_m
        if 12 <= i <= 14:
            lo, hi = {12: (0, 6), 13: (6, 11), 14: (11, 16)}[i]
            for dc in range(lo, hi):
                sl = slice(dc * TOK, (dc + 1) * TOK)
                u1 = S.op("vector", lambda e, sl=sl, dc=dc: e.scalar_tensor_tensor(
                    OUTB[:, sl], XN[:, sl], APP[:, dc:dc + 1], AUX1[:, :],
                    ALU.mult, ALU.add),
                    deps=[state["t_app"]] + state["ucopies"])
                u2 = S.op("vector", lambda e, sl=sl: e.tensor_tensor(
                    TBL[:, 0:1024], XN[:, sl], AUX0[:, :], ALU.mult),
                    deps=[state.get("ulast"), tbl_state["free"]])
                u3 = S.op("vector", lambda e, sl=sl: e.tensor_tensor(
                    OUTB[:, sl], OUTB[:, sl], TBL[:, 0:1024], ALU.add),
                    deps=[u1, u2])
                state["utok"][dc] = u3
                state["ulast"] = u3
        if i == 14:
            for dc in range(16):
                sl = slice(dc * TOK, (dc + 1) * TOK)
                state["o2a"][dc] = S.op(
                    "vector", lambda e, sl=sl, dc=dc: e.scalar_tensor_tensor(
                        OUTB[:, sl], ACC[:, sl], SDT[:, dc:dc + 1],
                        OUTB[:, sl], ALU.mult, ALU.add),
                    deps=[state["acc_last"][dc][0], state["acc_last"][dc][1],
                          t_sd, state["utok"][dc]])

    ost_free = [[], [], [], []]
    out_toks = []
    PBB = PB[0][:, :].bitcast(BF16)
    tb_ap = [PT[0][:, :], PT[1][:, :], PBB[:, 0:1024], PBB[:, 1024:2048]]
    tb_free = [state["ucopies"][0], state["ucopies"][1],
               S.barrier_tok("vector"), S.barrier_tok("vector")]
    OUTQ = ["sync", "gpsimd"]
    for tc in range(8):
        ob = tc % 4
        adds = []
        for b in range(2):
            tb = (tc % 2) * 2 + b
            for j in range(8):
                dc = b * 8 + j
                def tpf(e, dc=dc, j=j, tb=tb, tc=tc):
                    return e.matmul(tb_ap[tb][:, j * 128:(j + 1) * 128],
                                    OUTB[:, dc * TOK + tc * 128: dc * TOK + tc * 128 + 128],
                                    IDB[:, :], start=(j == 0), stop=(j == 7),
                                    is_transpose=True)
                deps = ([o2_tok[dc2][tc // 4] for dc2 in range(b * 8, b * 8 + 8)]
                        + [tb_free[tb], t_idb]) if j == 0 else ()
                tp = S.op("tensor", tpf, deps=deps, inc=(j == 7))
            def cpo(e, tb=tb, b=b, ob=ob):
                return e.tensor_copy(OST[ob][:, b * 1024:(b + 1) * 1024], tb_ap[tb])
            def cpo_s(e, tb=tb, b=b, ob=ob):
                return e.copy(OST[ob][:, b * 1024:(b + 1) * 1024], tb_ap[tb])
            atok = S.op("vector" if b == 0 else "scalar",
                        cpo if b == 0 else cpo_s,
                        deps=[tp] + list(ost_free[ob]))
            tb_free[tb] = atok
            adds.append(atok)
        ot = S.dma(OUTQ[tc % 2], lambda e, tc=tc, ob=ob: e.dma_start(
            y_h[tc * 128:(tc + 1) * 128, :], OST[ob][:, :]), deps=adds)
        out_toks.append(ot)
        ost_free[ob] = [ot]

    S.wait_only("sync", out_toks)

    with ctx:
        S.emit(nc)
    return nc


_NC = None


def _rope_tables():
    rot_dim = 1024
    pos = np.arange(2048, dtype=np.float32)
    inv_freq = (1.0 / (10000.0 ** (np.arange(0, rot_dim, 2, dtype=np.float32) / np.float32(rot_dim)))).astype(np.float32)
    freqs = pos[:, None] * inv_freq[None, :]
    freqs = np.repeat(freqs, 2, axis=-1)
    cos = np.cos(freqs.astype(np.float64)).astype(np.float32)
    sin = np.sin(freqs.astype(np.float64)).astype(np.float32)
    sinT = sin.T.copy()
    sinT[0::2, :] *= -1.0
    return cos.T.copy(), sinT


def _jt_matrix():
    m = np.zeros((128, 128), dtype=ml_dtypes.bfloat16)
    for i in range(64):
        m[2 * i, 2 * i + 1] = 1.0
        m[2 * i + 1, 2 * i] = 1.0
    return m.T.copy()


def kernel(L0_attn_X, L0_attn_W, L0_attn_b, L0_ffn_w_gate, L0_ffn_w_up,
           L0_ffn_w_down, L0_ffn_s_gate, L0_ffn_s_up, L0_ffn_s_down):
    global _NC
    if _NC is None:
        _NC = _build_program()

    X = np.ascontiguousarray(np.asarray(L0_attn_X, dtype=np.float32))
    W = np.ascontiguousarray(np.asarray(L0_attn_W, dtype=np.float32))
    bv = np.asarray(L0_attn_b, dtype=np.float32)
    wg = np.ascontiguousarray(np.asarray(L0_ffn_w_gate, dtype=np.int8))
    wu = np.ascontiguousarray(np.asarray(L0_ffn_w_up, dtype=np.int8))
    wd = np.ascontiguousarray(np.asarray(L0_ffn_w_down, dtype=np.int8))
    sg = np.asarray(L0_ffn_s_gate, dtype=np.float32)
    su = np.asarray(L0_ffn_s_up, dtype=np.float32)
    sd = np.asarray(L0_ffn_s_down, dtype=np.float32)

    cosT, sinT = _rope_tables()
    jt = _jt_matrix()
    idb = np.eye(128, dtype=ml_dtypes.bfloat16)
    idf = np.eye(128, dtype=np.float32)
    bpp = bv.reshape(32, 128).T.copy()
    sgpp = sg.reshape(64, 128).T.copy()
    supp = su.reshape(64, 128).T.copy()
    sdpp = sd.reshape(16, 128).T.copy()

    in_maps = []
    for c in range(8):
        b, h = c // 2, c % 2
        in_maps.append(dict(
            x=np.ascontiguousarray(X[b, h * 1024:(h + 1) * 1024, :]),
            w=W, wg=wg, wu=wu, wd=wd,
            bpp=bpp, sgpp=sgpp, supp=supp, sdpp=sdpp,
            cost=np.ascontiguousarray(cosT[:, h * 1024:(h + 1) * 1024]),
            sint=np.ascontiguousarray(sinT[:, h * 1024:(h + 1) * 1024]),
            idb=idb, idf=idf, jt=jt,
        ))

    res = run_bass_kernel_spmd(_NC, in_maps, core_ids=list(range(8)))
    out = np.empty((4, 2048, 2048), dtype=np.float32)
    for c in range(8):
        b, h = c // 2, c % 2
        out[b, h * 1024:(h + 1) * 1024, :] = np.asarray(res.results[c]["y"])
    return out
